# version 04
# baseline (speedup 1.0000x reference)
import math
from contextlib import ExitStack

import numpy as np
import ml_dtypes

import concourse.bass as bass
import concourse.mybir as mybir
from concourse.bass_utils import run_bass_kernel_spmd

F32 = mybir.dt.float32
BF16 = mybir.dt.bfloat16
I32 = mybir.dt.int32
AF = mybir.ActivationFunctionType
ALU = mybir.AluOpType

S, E = 8192, 4096
HQ, KVH, HD = 32, 8, 128
WIN = 1024
FF = 11008
NCORE = 8
TPC = S // NCORE
NT = TPC // 128
DC = E // 128
FC = FF // 128
NKT = S // 128
CSHIFT = 60.0
ISQ = 1.0 / math.sqrt(HD)
EPS = 1e-5


class Sem:
    def __init__(self, handle):
        self.h = handle
        self.n = 0


class Prog:

    ENGS = ("tensor", "scalar", "vector", "gpsimd", "sync")

    def __init__(self):
        self.ops = {e: [] for e in self.ENGS}
        self.waited = {e: {} for e in self.ENGS}

    def emit(self, eng, fn, waits=(), inc=None, inc_n=1):
        real_waits = []
        for (s, v) in waits:
            if v <= 0:
                continue
            prev = self.waited[eng].get(id(s), -1)
            if v > prev:
                real_waits.append((s, v))
                self.waited[eng][id(s)] = v
        if inc is not None:
            inc.n += inc_n

        def run(engine, fn=fn, real_waits=tuple(real_waits), inc=inc, inc_n=inc_n):
            for (s, v) in real_waits:
                engine.wait_ge(s.h, v)
            instr = fn(engine)
            if inc is not None:
                instr.then_inc(inc.h, inc_n)

        self.ops[eng].append(run)


def build_nc():
    nc = bass.Bass("TRN2", target_bir_lowering=False)

    x_in = nc.dram_tensor("x", [TPC, E], F32, kind="ExternalInput").ap()
    wq_in = nc.dram_tensor("wq", [HQ, E, 128], F32, kind="ExternalInput").ap()
    wk_in = nc.dram_tensor("wk", [KVH, E, 128], F32, kind="ExternalInput").ap()
    wv_in = nc.dram_tensor("wv", [E, KVH * 128], F32, kind="ExternalInput").ap()
    wo_in = nc.dram_tensor("wo", [8, E, 512], F32, kind="ExternalInput").ap()
    gw_in = nc.dram_tensor("gw", [FC, E, 128], I32, kind="ExternalInput").ap()
    uw_in = nc.dram_tensor("uw", [FC, E, 128], I32, kind="ExternalInput").ap()
    dw_in = nc.dram_tensor("dw", [8, FF, 512], I32, kind="ExternalInput").ap()
    gs_in = nc.dram_tensor("gs", [128, FC], F32, kind="ExternalInput").ap()
    us_in = nc.dram_tensor("us", [128, FC], F32, kind="ExternalInput").ap()
    dsb_in = nc.dram_tensor("dsb", [128, E], F32, kind="ExternalInput").ap()
    cos_in = nc.dram_tensor("cos64", [64, TPC], F32, kind="ExternalInput").ap()
    sin_in = nc.dram_tensor("sin64", [64, TPC], F32, kind="ExternalInput").ap()
    msk_in = nc.dram_tensor("masks", [2, NKT, 128, 512], BF16, kind="ExternalInput").ap()
    y_out = nc.dram_tensor("y", [TPC, E], F32, kind="ExternalOutput").ap()

    kself = nc.dram_tensor("kself", [KVH * 128, TPC], BF16).ap()
    kg = nc.dram_tensor("kg", [NCORE * KVH * 128, TPC], BF16).ap()
    vself = nc.dram_tensor("vself", [NT, KVH, 128, 128], BF16).ap()
    vg = nc.dram_tensor("vg", [NCORE, NT, KVH, 128, 128], BF16).ap()
    qbuf = nc.dram_tensor("qbuf", [HQ, 128, TPC], BF16).ap()
    abuf = nc.dram_tensor("abuf", [HQ, 128, TPC], BF16).ap()
    hbuf = nc.dram_tensor("hbuf", [FC, 128, TPC], BF16).ap()

    cos_s = nc.alloc_sbuf_tensor("cos_s", [64, TPC], F32).ap()
    sin_s = nc.alloc_sbuf_tensor("sin_s", [64, TPC], F32).ap()
    gs_s = nc.alloc_sbuf_tensor("gs_s", [128, FC], F32).ap()
    us_s = nc.alloc_sbuf_tensor("us_s", [128, FC], F32).ap()
    dsb_s = nc.alloc_sbuf_tensor("dsb_s", [128, E], F32).ap()
    ident = nc.alloc_sbuf_tensor("ident", [128, 128], BF16).ap()
    ones_col = nc.alloc_sbuf_tensor("ones_col", [128, 1], BF16).ap()
    ones_row = nc.alloc_sbuf_tensor("ones_row", [1, 128], F32).ap()
    c_scale = nc.alloc_sbuf_tensor("c_scale", [128, 1], F32).ap()
    c_bias = nc.alloc_sbuf_tensor("c_bias", [128, 1], F32).ap()
    c_eps = nc.alloc_sbuf_tensor("c_eps", [128, 1], F32).ap()
    ln_xsum = nc.alloc_sbuf_tensor("ln_xsum", [128, 2], F32).ap()
    ln_ssum = nc.alloc_sbuf_tensor("ln_ssum", [128, 2], F32).ap()
    ln_mean = nc.alloc_sbuf_tensor("ln_mean", [128, 2], F32).ap()
    ln_var = nc.alloc_sbuf_tensor("ln_var", [128, 2], F32).ap()
    ln_std = nc.alloc_sbuf_tensor("ln_std", [128, 2], F32).ap()
    ln_rstd = nc.alloc_sbuf_tensor("ln_rstd", [128, 2], F32).ap()
    ln_nbias = nc.alloc_sbuf_tensor("ln_nbias", [128, 2], F32).ap()
    ln_t = nc.alloc_sbuf_tensor("ln_t", [128, 2], F32).ap()

    BASE = (nc.sbuf_base + 31) & ~31
    TOP = nc.sbuf_top
    avail = TOP - BASE
    assert avail >= 180 * 1024, f"overlay too small: {avail}"

    made = {}

    def at(name, off_bytes, shape, dtype):
        key = (name, off_bytes)
        if key not in made:
            nbytes = int(np.prod(shape[1:])) * mybir.dt.size(dtype)
            assert off_bytes % 32 == 0
            assert off_bytes + nbytes <= avail, (name, off_bytes, nbytes, avail)
            made[key] = nc.alloc_sbuf_tensor_at(
                f"{name}_{off_bytes}", shape, dtype, offset=BASE + off_bytes).ap()
        return made[key]

    KB = 1024

    p = Prog()

    def mksem(name):
        return Sem(nc.semaphore(name).__enter__())

    sCC = mksem("sCC")
    sPE = mksem("sPE")
    sACT = mksem("sACT")
    sDVE = mksem("sDVE")
    sPOOL = mksem("sPOOL")
    sConst = mksem("sConst")
    sXT = [mksem("sXT0"), mksem("sXT1")]
    sWS = [mksem("sWS0"), mksem("sWS1")]
    sGS = mksem("sGS")
    sUS = mksem("sUS")
    sKSL = [mksem("sKSL0"), mksem("sKSL1")]
    sVSL = [mksem("sVSL0"), mksem("sVSL1")]
    sQSB = [mksem("sQSB0"), mksem("sQSB1")]
    sMSK = mksem("sMSK")
    sST = [mksem("sST0"), mksem("sST1")]
    sSTB = mksem("sSTB")
    ALL_SEMS = (sCC, sPE, sACT, sDVE, sPOOL, sConst, sXT[0], sXT[1],
                sWS[0], sWS[1], sGS, sUS, sKSL[0], sKSL[1], sVSL[0], sVSL[1],
                sQSB[0], sQSB[1], sMSK, sST[0], sST[1], sSTB)

    def snap():
        return [(s, s.n) for s in ALL_SEMS]

    p.emit("sync", lambda e: e.dma_start(out=cos_s, in_=cos_in), inc=sConst, inc_n=16)
    p.emit("sync", lambda e: e.dma_start(out=sin_s, in_=sin_in), inc=sConst, inc_n=16)
    p.emit("sync", lambda e: e.dma_start(out=gs_s, in_=gs_in), inc=sConst, inc_n=16)
    p.emit("sync", lambda e: e.dma_start(out=us_s, in_=us_in), inc=sConst, inc_n=16)
    p.emit("sync", lambda e: e.dma_start(out=dsb_s, in_=dsb_in), inc=sConst, inc_n=16)
    p.emit("vector", lambda e: e.memset(ident, 0.0))
    p.emit("vector", lambda e: e.memset(ones_col, 1.0))
    p.emit("vector", lambda e: e.memset(ones_row, 1.0))
    p.emit("vector", lambda e: e.memset(c_scale, ISQ))
    p.emit("vector", lambda e: e.memset(c_bias, -CSHIFT))
    p.emit("vector", lambda e: e.memset(c_eps, EPS), inc=sDVE)
    dve_const_done = sDVE.n
    p.emit("gpsimd", lambda e: e.affine_select(
        out=ident, in_=ident, compare_op=ALU.not_equal, fill=1.0,
        base=0, pattern=[[-1, 128]], channel_multiplier=1),
        waits=[(sDVE, dve_const_done)], inc=sPOOL)
    ident_done = sPOOL.n

    xtile = [at("xtile", i * 16 * KB, [128, E], F32) for i in range(2)]
    xsq = at("xsq", 32 * KB, [128, E], F32)
    xn_tm = [at("xn_tm", 48 * KB + i * 8 * KB, [128, E], BF16) for i in range(2)]
    xn_fm = at("xn_fm", 64 * KB, [128, DC, TPC], BF16)
    XN_OFF, XN_END = 64 * KB, 128 * KB

    with nc.psum_tensor("trp0", [128, 128], BF16) as trp0_t, \
         nc.psum_tensor("trp1", [128, 128], BF16) as trp1_t:
        trp = [trp0_t.ap(), trp1_t.ap()]
        xload_done = [0] * NT
        xn_done = [0] * NT
        red_done = [0] * NT
        tr_done = [0] * NT
        ev_done = [0] * (NT * DC)
        for it in range(NT):
            b = it % 2
            waits = []
            if it >= 2:
                waits = [(sDVE, xn_done[it - 2]), (sDVE, red_done[it - 2])]
            p.emit("sync", lambda e, b=b, it=it: e.dma_start(
                out=xtile[b], in_=x_in[it * 128:(it + 1) * 128, :]),
                waits=waits, inc=sXT[b], inc_n=16)
            xload_done[it] = (sXT[b], sXT[b].n)
            p.emit("scalar", lambda e, b=b: e.activation(
                out=xsq, in_=xtile[b], func=AF.Square),
                waits=[xload_done[it]] + ([(sDVE, red_done[it - 1])] if it >= 1 else []),
                inc=sACT)
            sq_done = sACT.n
            p.emit("vector", lambda e, b=b: e.reduce_sum(
                ln_xsum[:, b:b + 1], xtile[b], axis=mybir.AxisListType.X),
                waits=[xload_done[it]])
            p.emit("vector", lambda e, b=b: e.reduce_sum(
                ln_ssum[:, b:b + 1], xsq, axis=mybir.AxisListType.X),
                waits=[(sACT, sq_done)], inc=sDVE)
            red_done[it] = sDVE.n
            p.emit("vector", lambda e: e.drain())
            p.emit("vector", lambda e, b=b: e.tensor_scalar_mul(
                ln_mean[:, b:b + 1], ln_xsum[:, b:b + 1], 1.0 / E))
            p.emit("vector", lambda e, b=b: e.tensor_scalar_mul(
                ln_t[:, b:b + 1], ln_ssum[:, b:b + 1], 1.0 / E))
            p.emit("vector", lambda e: e.drain())
            p.emit("vector", lambda e, b=b: e.scalar_tensor_tensor(
                out=ln_var[:, b:b + 1], in0=ln_mean[:, b:b + 1], scalar=-1.0,
                in1=ln_mean[:, b:b + 1], op0=ALU.mult, op1=ALU.mult))
            p.emit("vector", lambda e: e.drain())
            p.emit("vector", lambda e, b=b: e.tensor_add(
                ln_var[:, b:b + 1], ln_var[:, b:b + 1], ln_t[:, b:b + 1]), inc=sDVE)
            var_done = sDVE.n
            p.emit("scalar", lambda e, b=b: e.activation(
                out=ln_std[:, b:b + 1], in_=ln_var[:, b:b + 1], func=AF.Sqrt,
                bias=c_eps[:, 0:1], scale=1.0),
                waits=[(sDVE, var_done)], inc=sACT)
            std_done = sACT.n
            p.emit("vector", lambda e, b=b: e.reciprocal(
                ln_rstd[:, b:b + 1], ln_std[:, b:b + 1]),
                waits=[(sACT, std_done)])
            p.emit("vector", lambda e: e.drain())
            p.emit("vector", lambda e, b=b: e.scalar_tensor_tensor(
                out=ln_nbias[:, b:b + 1], in0=ln_mean[:, b:b + 1], scalar=-1.0,
                in1=ln_rstd[:, b:b + 1], op0=ALU.mult, op1=ALU.mult), inc=sDVE)
            nb_done = sDVE.n
            p.emit("vector", lambda e: e.drain())
            xn_waits = [(sDVE, nb_done)]
            if it >= 2:
                xn_waits.append((sPE, tr_done[it - 2]))
            p.emit("vector", lambda e, b=b: e.tensor_scalar(
                out=xn_tm[b], in0=xtile[b],
                scalar1=ln_rstd[:, b:b + 1], scalar2=ln_nbias[:, b:b + 1],
                op0=ALU.mult, op1=ALU.add),
                waits=xn_waits, inc=sDVE)
            xn_done[it] = sDVE.n
            for d in range(DC):
                tw = [(sDVE, xn_done[it]), (sPOOL, ident_done)]
                prev = it * DC + d - 2
                if prev >= 0:
                    tw.append((sDVE, ev_done[prev]))
                p.emit("tensor", lambda e, b=b, d=d: e.transpose(
                    trp[d % 2], xn_tm[b][:, d * 128:(d + 1) * 128], ident),
                    waits=tw, inc=sPE)
                tr_done[it] = sPE.n
                p.emit("vector", lambda e, it=it, d=d: e.tensor_copy(
                    out=xn_fm[:, d, it * 128:(it + 1) * 128], in_=trp[d % 2]),
                    waits=[(sPE, sPE.n)], inc=sDVE)
                ev_done[it * DC + d] = sDVE.n
        xn_ready = sDVE.n

    V_tm = at("V_tm", 0, [128, NT, TPC], BF16)
    K_fm = at("K_fm", 16 * KB, [128, KVH, TPC], BF16)
    wstage = [at("wstage", 32 * KB + i * 16 * KB, [128, 8, 512], F32) for i in range(2)]
    wv_bf = at("wv_bf", 128 * KB, [128, DC, 512], BF16)
    wkstage = [at("wkstage", 32 * KB + i * 16 * KB, [128, DC, 128], F32) for i in range(2)]
    wk_bf = [at("wk_bf", 128 * KB + i * 8 * KB, [128, DC, 128], BF16) for i in range(2)]
    rsc = [at("rsc", 160 * KB + i * 2 * KB, [64, 512], F32) for i in range(4)]

    s2bar = snap()
    with nc.psum_tensor("pv", [128, 2, 512], F32) as pv_t, \
         nc.psum_tensor("pk", [128, 2, 2, 512], F32) as pk_t:
        pv = pv_t.ap()
        pk = pk_t.ap()
        vconv_done = [0] * 8
        vmm_done = [0] * (2 * NT)
        vevac_done = [0] * (2 * NT)
        for vc in range(2):
            for dg in range(4):
                b = dg % 2
                w = list(s2bar) if (vc == 0 and dg < 2) else []
                if vc * 4 + dg >= 2:
                    w.append((sACT, vconv_done[vc * 4 + dg - 2]))
                p.emit("sync", lambda e, b=b, vc=vc, dg=dg: e.dma_start(
                    out=wstage[b],
                    in_=wv_in.rearrange("(c p) n -> p c n", p=128)[:, dg * 8:(dg + 1) * 8, vc * 512:(vc + 1) * 512]),
                    waits=w, inc=sWS[b], inc_n=16)
                p.emit("scalar", lambda e, b=b, dg=dg: e.activation(
                    out=wv_bf[:, dg * 8:(dg + 1) * 8, :], in_=wstage[b], func=AF.Copy),
                    waits=[(sWS[b], sWS[b].n)] + (list(s2bar) if (vc == 0 and dg == 0) else []),
                    inc=sACT)
                vconv_done[vc * 4 + dg] = sACT.n
            for it in range(NT):
                b = it % 2
                idx = vc * NT + it
                mw = [(sDVE, xn_ready), (sACT, vconv_done[vc * 4 + 3])]
                if idx >= 2:
                    mw.append((sACT, vevac_done[idx - 2]))
                for d in range(DC):
                    p.emit("tensor", lambda e, b=b, it=it, d=d: e.matmul(
                        pv[:, b, :], xn_fm[:, d, it * 128:(it + 1) * 128],
                        wv_bf[:, d, :], start=(d == 0), stop=(d == DC - 1)),
                        waits=mw if d == 0 else (), inc=(sPE if d == DC - 1 else None))
                vmm_done[idx] = sPE.n
                p.emit("scalar", lambda e, b=b, it=it, vc=vc: e.activation(
                    out=V_tm[:, it, vc * 512:(vc + 1) * 512], in_=pv[:, b, :], func=AF.Copy),
                    waits=[(sPE, vmm_done[idx])], inc=sACT)
                vevac_done[idx] = sACT.n
        for kt2 in range(NT):
            p.emit("gpsimd", lambda e, kt2=kt2: e.dma_start(
                out=vself[kt2].rearrange("g p v -> p g v"),
                in_=V_tm[:, kt2, :].rearrange("p (g v) -> p g v", g=KVH)),
                waits=[(sACT, vevac_done[NT + kt2])], inc=sSTB, inc_n=16)
        vstore_done = sSTB.n
        kconv_done = [0] * KVH
        kmm_done = [0] * KVH
        krope_done = [0] * KVH
        for kc in range(KVH):
            b = kc % 2
            w = [(sACT, vconv_done[7])] if kc < 2 else [(sACT, kconv_done[kc - 2])]
            p.emit("sync", lambda e, b=b, kc=kc: e.dma_start(
                out=wkstage[b], in_=wk_in[kc].rearrange("(c p) m -> p c m", p=128)),
                waits=w, inc=sWS[b], inc_n=16)
            cw = [(sWS[b], sWS[b].n)]
            if kc >= 2:
                cw.append((sPE, kmm_done[kc - 2]))
            p.emit("scalar", lambda e, b=b: e.activation(
                out=wk_bf[b], in_=wkstage[b], func=AF.Copy), waits=cw, inc=sACT)
            kconv_done[kc] = sACT.n
            mw = [(sACT, kconv_done[kc]), (sDVE, xn_ready)]
            if kc >= 2:
                mw.append((sDVE, krope_done[kc - 2]))
            for tt in range(2):
                for d in range(DC):
                    p.emit("tensor", lambda e, b=b, tt=tt, d=d: e.matmul(
                        pk[:, b, tt, :], wk_bf[b][:, d, :],
                        xn_fm[:, d, tt * 512:(tt + 1) * 512],
                        start=(d == 0), stop=(d == DC - 1)),
                        waits=mw if (tt == 0 and d == 0) else (),
                        inc=(sPE if (tt == 1 and d == DC - 1) else None))
            kmm_done[kc] = sPE.n
            for tt in range(2):
                cosl = cos_s[:, tt * 512:(tt + 1) * 512]
                sinl = sin_s[:, tt * 512:(tt + 1) * 512]
                sl = slice(tt * 512, (tt + 1) * 512)
                p.emit("vector", lambda e, b=b, tt=tt, cosl=cosl: e.tensor_mul(
                    rsc[0], pk[0:64, b, tt, :], cosl),
                    waits=[(sPE, kmm_done[kc]), (sConst, 5 * 16)])
                p.emit("vector", lambda e, b=b, tt=tt, sinl=sinl: e.tensor_mul(
                    rsc[1], pk[64:128, b, tt, :], sinl))
                p.emit("vector", lambda e: e.drain())
                p.emit("vector", lambda e, kc=kc, sl=sl: e.tensor_sub(
                    K_fm[0:64, kc, sl], rsc[0], rsc[1]))
                p.emit("vector", lambda e, b=b, tt=tt, cosl=cosl: e.tensor_mul(
                    rsc[2], pk[64:128, b, tt, :], cosl))
                p.emit("vector", lambda e, b=b, tt=tt, sinl=sinl: e.tensor_mul(
                    rsc[3], pk[0:64, b, tt, :], sinl))
                p.emit("vector", lambda e: e.drain())
                p.emit("vector", lambda e, kc=kc, sl=sl: e.tensor_add(
                    K_fm[64:128, kc, sl], rsc[2], rsc[3]),
                    inc=(sDVE if tt == 1 else None))
            krope_done[kc] = sDVE.n
        p.emit("gpsimd", lambda e: e.dma_start(
            out=kself.rearrange("(g p) t -> p g t", p=128), in_=K_fm),
            waits=[(sDVE, krope_done[KVH - 1])], inc=sSTB, inc_n=16)
        kvstore_total = sSTB.n
        p.emit("gpsimd", lambda e: e.collective_compute(
            "AllGather", ALU.bypass,
            replica_groups=[list(range(NCORE))],
            ins=[kself.opt()], outs=[kg.opt()]),
            waits=[(sSTB, kvstore_total)], inc=sCC)
        p.emit("gpsimd", lambda e: e.collective_compute(
            "AllGather", ALU.bypass,
            replica_groups=[list(range(NCORE))],
            ins=[vself.rearrange("a g p v -> (a g p) v").opt()],
            outs=[vg.rearrange("r a g p v -> (r a g p) v").opt()]),
            waits=[(sSTB, kvstore_total)], inc=sCC)
        cc_done = sCC.n

    qchunk = [at("qchunk", 176 * KB + i * 2 * KB, [128, TPC], BF16) for i in range(2)]
    s3bar = snap()
    with nc.psum_tensor("pq", [128, 2, 2, 512], F32) as pq_t:
        pq = pq_t.ap()
        qconv_done = [0] * HQ
        qmm_done = [0] * HQ
        qst_done = [0] * HQ
        qrope_done = [0] * HQ
        for qc in range(HQ):
            b = qc % 2
            w = [(sACT, qconv_done[qc - 2])] if qc >= 2 else list(s3bar)
            p.emit("sync", lambda e, b=b, qc=qc: e.dma_start(
                out=wkstage[b], in_=wq_in[qc].rearrange("(c p) m -> p c m", p=128)),
                waits=w, inc=sWS[b], inc_n=16)
            cw = [(sWS[b], sWS[b].n)]
            if qc >= 2:
                cw.append((sPE, qmm_done[qc - 2]))
            else:
                cw += list(s3bar)
            p.emit("scalar", lambda e, b=b: e.activation(
                out=wk_bf[b], in_=wkstage[b], func=AF.Copy), waits=cw, inc=sACT)
            qconv_done[qc] = sACT.n
            mw = [(sACT, qconv_done[qc])]
            if qc >= 2:
                mw.append((sDVE, qrope_done[qc - 2]))
            for tt in range(2):
                for d in range(DC):
                    p.emit("tensor", lambda e, b=b, tt=tt, d=d: e.matmul(
                        pq[:, b, tt, :], wk_bf[b][:, d, :],
                        xn_fm[:, d, tt * 512:(tt + 1) * 512],
                        start=(d == 0), stop=(d == DC - 1)),
                        waits=mw if (tt == 0 and d == 0) else (),
                        inc=(sPE if (tt == 1 and d == DC - 1) else None))
            qmm_done[qc] = sPE.n
            rw = [(sPE, qmm_done[qc])]
            if qc >= 2:
                rw.append(qst_done[qc - 2])
            for tt in range(2):
                cosl = cos_s[:, tt * 512:(tt + 1) * 512]
                sinl = sin_s[:, tt * 512:(tt + 1) * 512]
                sl = slice(tt * 512, (tt + 1) * 512)
                p.emit("vector", lambda e, b=b, tt=tt, cosl=cosl: e.tensor_mul(
                    rsc[0], pq[0:64, b, tt, :], cosl), waits=rw if tt == 0 else ())
                p.emit("vector", lambda e, b=b, tt=tt, sinl=sinl: e.tensor_mul(
                    rsc[1], pq[64:128, b, tt, :], sinl))
                p.emit("vector", lambda e: e.drain())
                p.emit("vector", lambda e, b=b, sl=sl: e.tensor_sub(
                    qchunk[b][0:64, sl], rsc[0], rsc[1]))
                p.emit("vector", lambda e, b=b, tt=tt, cosl=cosl: e.tensor_mul(
                    rsc[2], pq[64:128, b, tt, :], cosl))
                p.emit("vector", lambda e, b=b, tt=tt, sinl=sinl: e.tensor_mul(
                    rsc[3], pq[0:64, b, tt, :], sinl))
                p.emit("vector", lambda e: e.drain())
                p.emit("vector", lambda e, b=b, sl=sl: e.tensor_add(
                    qchunk[b][64:128, sl], rsc[2], rsc[3]),
                    inc=(sDVE if tt == 1 else None))
            qrope_done[qc] = sDVE.n
            p.emit("gpsimd", lambda e, b=b, qc=qc: e.dma_start(
                out=qbuf[qc], in_=qchunk[b]),
                waits=[(sDVE, qrope_done[qc])], inc=sST[b], inc_n=16)
            qst_done[qc] = (sST[b], sST[b].n)
        qspill_done = [(sST[0], sST[0].n), (sST[1], sST[1].n)]

    gstage = at("gstage", 0, [128, DC, 128], I32)
    ustage = at("ustage", 16 * KB, [128, DC, 128], I32)
    g_bf = [at("g_bf", 32 * KB + i * 8 * KB, [128, DC, 128], BF16) for i in range(2)]
    u_bf = [at("u_bf", 48 * KB + i * 8 * KB, [128, DC, 128], BF16) for i in range(2)]
    tg_s = at("tg_s", 168 * KB, [128, TPC], BF16)
    tu_s = at("tu_s", 172 * KB, [128, TPC], BF16)
    h_t = [at("h_t", 176 * KB + i * 2 * KB, [128, TPC], BF16) for i in range(2)]
    s4bar = snap()
    with nc.psum_tensor("pg", [128, TPC], F32) as pg_t, \
         nc.psum_tensor("pu", [128, TPC], F32) as pu_t:
        pg = pg_t.ap()
        pu = pu_t.ap()
        gconv_done = [0] * FC
        uconv_done = [0] * FC
        fmm_done = [0] * FC
        hmul_done = [0] * FC
        hst_done = [0] * FC
        for f in range(FC):
            b = f % 2
            lw = [(sDVE, gconv_done[f - 1])] if f >= 1 else list(s4bar)
            p.emit("sync", lambda e, f=f: e.dma_start(
                out=gstage, in_=gw_in[f].rearrange("(c p) m -> p c m", p=128)),
                waits=lw, inc=sGS, inc_n=16)
            gload = (sGS, sGS.n)
            uw_w = [(sACT, uconv_done[f - 1])] if f >= 1 else list(s4bar)
            p.emit("sync", lambda e, f=f: e.dma_start(
                out=ustage, in_=uw_in[f].rearrange("(c p) m -> p c m", p=128)),
                waits=uw_w, inc=sUS, inc_n=16)
            uload = (sUS, sUS.n)
            gw_w = [gload]
            uw_w2 = [uload]
            if f >= 2:
                gw_w.append((sPE, fmm_done[f - 2]))
                uw_w2.append((sPE, fmm_done[f - 2]))
            if f < 2:
                gw_w += list(s4bar)
                uw_w2 += list(s4bar)
            p.emit("vector", lambda e, b=b: e.tensor_copy(out=g_bf[b], in_=gstage),
                   waits=gw_w, inc=sDVE)
            gconv_done[f] = sDVE.n
            p.emit("scalar", lambda e, b=b: e.activation(
                out=u_bf[b], in_=ustage, func=AF.Copy), waits=uw_w2, inc=sACT)
            uconv_done[f] = sACT.n
            mw = [(sDVE, gconv_done[f]), (sACT, uconv_done[f])]
            if f >= 1:
                mw.append((sDVE, hmul_done[f - 1]))
            for d in range(DC):
                for tt in range(2):
                    p.emit("tensor", lambda e, b=b, d=d, tt=tt: e.matmul(
                        pg[:, tt * 512:(tt + 1) * 512], g_bf[b][:, d, :],
                        xn_fm[:, d, tt * 512:(tt + 1) * 512],
                        start=(d == 0), stop=(d == DC - 1), skip_group_check=True),
                        waits=mw if (d == 0 and tt == 0) else ())
            for d in range(DC):
                for tt in range(2):
                    p.emit("tensor", lambda e, b=b, d=d, tt=tt: e.matmul(
                        pu[:, tt * 512:(tt + 1) * 512], u_bf[b][:, d, :],
                        xn_fm[:, d, tt * 512:(tt + 1) * 512],
                        start=(d == 0), stop=(d == DC - 1), skip_group_check=True),
                        inc=(sPE if (d == DC - 1 and tt == 1) else None))
            fmm_done[f] = sPE.n
            ew = [(sPE, fmm_done[f])]
            if f >= 2:
                ew.append(hst_done[f - 2])
            p.emit("scalar", lambda e, f=f: e.activation(
                out=tg_s, in_=pg, func=AF.Silu, scale=gs_s[:, f:f + 1]),
                waits=ew, inc=sACT)
            tg_done = sACT.n
            p.emit("vector", lambda e, b=b, f=f: e.scalar_tensor_tensor(
                out=h_t[b], in0=pu, scalar=us_s[:, f:f + 1], in1=tg_s,
                op0=ALU.mult, op1=ALU.mult),
                waits=[(sACT, tg_done), (sPE, fmm_done[f])], inc=sDVE)
            hmul_done[f] = sDVE.n
            p.emit("gpsimd", lambda e, b=b, f=f: e.dma_start(
                out=hbuf[f], in_=h_t[b]),
                waits=[(sDVE, hmul_done[f])], inc=sST[b], inc_n=16)
            hst_done[f] = (sST[b], sST[b].n)
        hall_done = [(sST[0], sST[0].n), (sST[1], sST[1].n)]

    masks_s = at("masks_s", 0, [128, NKT, 512], BF16)
    kslab = [at("kslab", 64 * KB + i * 16 * KB, [128, NCORE, TPC], BF16) for i in range(2)]
    vslab = [at("vslab", 96 * KB + i * 16 * KB, [128, NKT, 128], BF16) for i in range(2)]
    qsb = [at("qsb", 128 * KB + i * KB, [128, 512], BF16) for i in range(2)]
    ptb = [at("ptb", 130 * KB + i * 2 * KB, [128, 2, 512], BF16) for i in range(2)]
    bc_s = [at("bc_s", 136 * KB + i * 2 * KB, [128, 512], F32) for i in range(2)]
    rec_s = [at("rec_s", 140 * KB + i * 2 * KB, [1, 512], F32) for i in range(2)]
    aout = [at("aout", 144 * KB + i * KB, [128, 512], BF16) for i in range(2)]

    s5bar = snap()
    with nc.psum_tensor("stp", [128, 2, 2, 512], F32) as stp_t, \
         nc.psum_tensor("avp", [128, 512], F32) as avp_t, \
         nc.psum_tensor("dnp", [1, 512], F32) as dnp_t, \
         nc.psum_tensor("bcp", [128, 512], F32) as bcp_t:
        stp = stp_t.ap()
        avp = avp_t.ap()
        dnp = dnp_t.ap()
        bcp = bcp_t.ap()
        mask_ld = [0, 0]
        slab_ld = {}
        q_ld = {}
        exp_done = {}
        av_pair_done = {}
        exp_done_last_idx = {}
        score_done = {}
        hfin = {}
        for qt in range(2):
            mw = list(s5bar) if qt == 0 else [(sPE, sPE.n)]
            for half in range(2):
                p.emit("gpsimd", lambda e, qt=qt, half=half: e.dma_start(
                    out=masks_s[:, half * 32:(half + 1) * 32, :],
                    in_=msk_in[qt].rearrange("k p q -> p k q")[:, half * 32:(half + 1) * 32, :]),
                    waits=mw if half == 0 else (), inc=sMSK, inc_n=16)
            mask_ld[qt] = (sMSK, sMSK.n)
            for g in range(KVH):
                b = g % 2
                sw = [(sCC, cc_done)]
                if qt == 0 and g < 2:
                    sw += list(s5bar)
                key_prev = (qt, g - 2) if g >= 2 else ((qt - 1, g + KVH - 2) if qt > 0 else None)
                if key_prev in score_done:
                    sw.append((sPE, score_done[key_prev]))
                p.emit("gpsimd", lambda e, g=g, b=b: e.dma_start(
                    out=kslab[b],
                    in_=kg.rearrange("(r gg p) t -> gg p r t", gg=KVH, p=128)[g]),
                    waits=sw, inc=sKSL[b], inc_n=16)
                for half in range(2):
                    p.emit("gpsimd", lambda e, g=g, b=b, half=half: e.dma_start(
                        out=vslab[b][:, half * 32:(half + 1) * 32, :],
                        in_=vg.rearrange("r a gg p v -> gg p (r a) v")[g][:, half * 32:(half + 1) * 32, :]),
                        inc=sVSL[b], inc_n=16)
                slab_ld[(qt, g)] = ((sKSL[b], sKSL[b].n), (sVSL[b], sVSL[b].n))
                for hl in range(4):
                    h = 4 * g + hl
                    hb = h % 2
                    qw = list(qspill_done)
                    hkey = (qt, g, hl)
                    pkey = _prev_h(qt, g, hl, 2)
                    if pkey in score_done:
                        qw.append((sPE, score_done[pkey]))
                    p.emit("gpsimd", lambda e, h=h, hb=hb, qt=qt: e.dma_start(
                        out=qsb[hb], in_=qbuf[h][:, qt * 512:(qt + 1) * 512]),
                        waits=qw, inc=sQSB[hb], inc_n=16)
                    q_ld[hkey] = (sQSB[hb], sQSB[hb].n)
                    av_start_wait = []
                    p1 = _prev_h(qt, g, hl, 1)
                    if p1 in hfin:
                        av_start_wait.append((sDVE, hfin[p1][0]))
                    for ktp in range(NKT // 2):
                        pb = ktp % 2
                        scw = [slab_ld[(qt, g)][0], slab_ld[(qt, g)][1], q_ld[hkey]]
                        if ktp >= 2:
                            scw.append((sACT, exp_done[(qt, g, hl, ktp - 2)]))
                        elif p1 in exp_done_last_idx:
                            scw.append((sACT, exp_done_last_idx[p1][ktp % 2]))
                        for j in range(2):
                            kt = 2 * ktp + j
                            r, kt2 = kt // 8, kt % 8
                            p.emit("tensor", lambda e, pb=pb, j=j, r=r, kt2=kt2, hb=hb, b=b: e.matmul(
                                stp[:, pb, j, :], kslab[b][:, r, kt2 * 128:(kt2 + 1) * 128],
                                qsb[hb], start=True, stop=True, skip_group_check=True),
                                waits=scw if j == 0 else (),
                                inc=(sPE if j == 1 else None))
                        sc_pair = sPE.n
                        ew = [(sPE, sc_pair)]
                        if ktp >= 2:
                            ew.append((sPE, av_pair_done[(qt, g, hl, ktp - 2)]))
                        p.emit("scalar", lambda e, pb=pb: e.activation(
                            out=ptb[pb].rearrange("p a q -> p (a q)"),
                            in_=stp[:, pb].rearrange("p a q -> p (a q)"),
                            func=AF.Exp, scale=c_scale[:, 0:1], bias=c_bias[:, 0:1]),
                            waits=ew, inc=sACT)
                        exp_done[(qt, g, hl, ktp)] = sACT.n
                        p.emit("vector", lambda e, pb=pb, ktp=ktp: e.tensor_mul(
                            ptb[pb].rearrange("p a q -> p (a q)"),
                            ptb[pb].rearrange("p a q -> p (a q)"),
                            masks_s[:, 2 * ktp:2 * ktp + 2, :].rearrange("p a q -> p (a q)")),
                            waits=[(sACT, sACT.n), mask_ld[qt]], inc=sDVE)
                        mdone = sDVE.n
                        avw = [(sDVE, mdone)] + (av_start_wait if ktp == 0 else [])
                        for j in range(2):
                            kt = 2 * ktp + j
                            p.emit("tensor", lambda e, pb=pb, j=j, kt=kt, b=b: e.matmul(
                                avp, vslab[b][:, kt, :], ptb[pb][:, j, :],
                                start=(kt == 0), stop=(kt == NKT - 1),
                                skip_group_check=True),
                                waits=avw if j == 0 else ())
                            p.emit("tensor", lambda e, pb=pb, j=j, kt=kt: e.matmul(
                                dnp, ones_col, ptb[pb][:, j, :],
                                start=(kt == 0), stop=(kt == NKT - 1),
                                skip_group_check=True),
                                inc=(sPE if j == 1 else None))
                        av_pair_done[(qt, g, hl, ktp)] = sPE.n
                    score_done[hkey] = sPE.n
                    fw = [(sPE, sPE.n)]
                    p.emit("vector", lambda e, hb=hb: e.reciprocal(rec_s[hb], dnp),
                           waits=fw, inc=sDVE)
                    rec_done = sDVE.n
                    bw = [(sDVE, rec_done)]
                    if p1 in hfin:
                        bw.append((sACT, hfin[p1][3]))
                    p.emit("tensor", lambda e, hb=hb: e.matmul(
                        bcp, ones_row, rec_s[hb], start=True, stop=True,
                        skip_group_check=True),
                        waits=bw, inc=sPE)
                    bc_mm = sPE.n
                    p.emit("scalar", lambda e, hb=hb: e.activation(
                        out=bc_s[hb], in_=bcp, func=AF.Copy),
                        waits=[(sPE, bc_mm)], inc=sACT)
                    bc_ev = sACT.n
                    aow = [(sACT, bc_ev)]
                    if p1 in hfin:
                        aow.append(hfin[p1][4])
                    p.emit("vector", lambda e, hb=hb: e.tensor_mul(
                        aout[hb], avp, bc_s[hb]), waits=aow, inc=sDVE)
                    av_used = sDVE.n
                    p.emit("gpsimd", lambda e, h=h, hb=hb, qt=qt: e.dma_start(
                        out=abuf[h][:, qt * 512:(qt + 1) * 512], in_=aout[hb]),
                        waits=[(sDVE, av_used)], inc=sST[hb], inc_n=16)
                    ast = (sST[hb], sST[hb].n)
                    hfin[hkey] = (av_used, rec_done, bc_mm, bc_ev, ast)
                    exp_done_last_idx[hkey] = (exp_done[(qt, g, hl, 30)],
                                               exp_done[(qt, g, hl, 31)])
        attn_done = [(sST[0], sST[0].n), (sST[1], sST[1].n)]

    attn_proj = at("attn_proj", 0, [128, NT, E], BF16)
    aslab = [at("aslab", 64 * KB + i * 2 * KB, [128, TPC], BF16) for i in range(2)]
    wostage = [at("wostage", 68 * KB + i * 16 * KB, [128, 8, 512], F32) for i in range(2)]
    wo_bf = at("wo_bf", 100 * KB, [128, DC, 512], BF16)

    s6bar = snap()
    with nc.psum_tensor("wop", [128, NT, 512], F32) as wop_t:
        wop = wop_t.ap()
        woconv_done = [0] * (8 * 4)
        aslab_ld = [0] * (8 * DC)
        womm_done = [0] * 8
        woev_done = [0] * 8
        for e_blk in range(8):
            for dg in range(4):
                b = dg % 2
                idx = e_blk * 4 + dg
                w = [(sACT, woconv_done[idx - 2])] if idx >= 2 else list(s6bar)
                p.emit("sync", lambda e, e_blk=e_blk, dg=dg, b=b: e.dma_start(
                    out=wostage[b],
                    in_=wo_in[e_blk].rearrange("(c p) n -> p c n", p=128)[:, dg * 8:(dg + 1) * 8, :]),
                    waits=w, inc=sWS[b], inc_n=16)
                cw = [(sWS[b], sWS[b].n)]
                if e_blk > 0 and dg < 2:
                    cw.append((sPE, womm_done[e_blk - 1]))
                if e_blk == 0 and dg == 0:
                    cw += list(s6bar)
                p.emit("scalar", lambda e, dg=dg, b=b: e.activation(
                    out=wo_bf[:, dg * 8:(dg + 1) * 8, :], in_=wostage[b], func=AF.Copy),
                    waits=cw, inc=sACT)
                woconv_done[idx] = sACT.n
            for hd in range(DC):
                b = hd % 2
                aw = list(attn_done)
                if e_blk * DC + hd >= 2:
                    aw.append((sPE, aslab_ld[e_blk * DC + hd - 2]))
                if e_blk == 0 and hd < 2:
                    aw += list(s6bar)
                p.emit("gpsimd", lambda e, hd=hd, b=b: e.dma_start(
                    out=aslab[b], in_=abuf[hd]), waits=aw, inc=sVSL[b], inc_n=16)
                asl = (sVSL[b], sVSL[b].n)
                mw = [asl, (sACT, woconv_done[e_blk * 4 + hd // 8])]
                if hd == 0 and e_blk > 0:
                    mw.append((sACT, woev_done[e_blk - 1]))
                for tok in range(NT):
                    p.emit("tensor", lambda e, b=b, hd=hd, tok=tok: e.matmul(
                        wop[:, tok, :], aslab[b][:, tok * 128:(tok + 1) * 128],
                        wo_bf[:, hd, :], start=(hd == 0), stop=(hd == DC - 1),
                        skip_group_check=True),
                        waits=mw if tok == 0 else (),
                        inc=(sPE if tok == NT - 1 else None))
                aslab_ld[e_blk * DC + hd] = sPE.n
            womm_done[e_blk] = sPE.n
            for tok in range(NT):
                p.emit("scalar", lambda e, tok=tok, e_blk=e_blk: e.activation(
                    out=attn_proj[:, tok, e_blk * 512:(e_blk + 1) * 512],
                    in_=wop[:, tok, :], func=AF.Copy),
                    waits=[(sPE, womm_done[e_blk])],
                    inc=(sACT if tok == NT - 1 else None))
            woev_done[e_blk] = sACT.n

    hsl = [at("hsl", 64 * KB + i * 2 * KB, [128, TPC], BF16) for i in range(2)]
    dwstage = [at("dwstage", 68 * KB + i * 2 * KB, [128, 512], I32) for i in range(2)]
    dwb = [at("dwb", 72 * KB + i * KB, [128, 512], BF16) for i in range(2)]
    xsl = [at("xsl", 80 * KB + i * 16 * KB, [128, NT, 512], F32) for i in range(2)]
    yout = [at("yout", 112 * KB + i * 2 * KB, [128, 512], F32) for i in range(2)]

    s7bar = snap()
    with nc.psum_tensor("dop", [128, NT, 512], F32) as dop_t:
        dop = dop_t.ap()
        dconv_done = {}
        dmmrow_done = {}
        dmm_done = [0] * 8
        fin_done = {}
        yst_done = {}
        for o in range(8):
            ob = o % 2
            xw = [(sDVE, fin_done[(o - 2, NT - 1)])] if o >= 2 else list(s7bar)
            p.emit("sync", lambda e, o=o, ob=ob: e.dma_start(
                out=xsl[ob],
                in_=x_in.rearrange("(t p) d -> p t d", p=128)[:, :, o * 512:(o + 1) * 512]),
                waits=xw, inc=sXT[ob], inc_n=16)
            xsl_ld = (sXT[ob], sXT[ob].n)
            for f in range(FC):
                fb = f % 2
                idx = o * FC + f
                w = [(sACT, dconv_done[idx - 2])] if idx >= 2 else (list(s7bar) if o == 0 else [])
                dsem = sGS if fb == 0 else sUS
                p.emit("sync", lambda e, o=o, f=f, fb=fb: e.dma_start(
                    out=dwstage[fb], in_=dw_in[o][f * 128:(f + 1) * 128, :]),
                    waits=w, inc=dsem, inc_n=16)
                dld = (dsem, dsem.n)
                cw = [dld]
                if idx >= 2:
                    cw.append((sPE, dmmrow_done[idx - 2]))
                elif o == 0:
                    cw += list(s7bar)
                p.emit("scalar", lambda e, fb=fb: e.activation(
                    out=dwb[fb], in_=dwstage[fb], func=AF.Copy), waits=cw, inc=sACT)
                dconv_done[idx] = sACT.n
                hw = list(hall_done)
                if idx >= 2:
                    hw.append((sPE, dmmrow_done[idx - 2]))
                if o == 0 and f < 2:
                    hw += list(s7bar)
                p.emit("gpsimd", lambda e, f=f, fb=fb: e.dma_start(
                    out=hsl[fb], in_=hbuf[f]), waits=hw, inc=sQSB[fb], inc_n=16)
                hld = (sQSB[fb], sQSB[fb].n)
                mw = [(sACT, dconv_done[idx]), hld]
                if f == 0 and o > 0:
                    mw.append((sDVE, fin_done[(o - 1, NT - 1)]))
                for tok in range(NT):
                    p.emit("tensor", lambda e, fb=fb, tok=tok, f=f: e.matmul(
                        dop[:, tok, :], hsl[fb][:, tok * 128:(tok + 1) * 128],
                        dwb[fb], start=(f == 0), stop=(f == FC - 1),
                        skip_group_check=True),
                        waits=mw if tok == 0 else (),
                        inc=(sPE if tok == NT - 1 else None))
                dmmrow_done[idx] = sPE.n
            dmm_done[o] = sPE.n
            for tok in range(NT):
                tb = tok % 2
                fw = [(sPE, dmm_done[o]), xsl_ld]
                prevk = (o, tok - 2) if tok >= 2 else ((o - 1, NT + tok - 2) if o > 0 else None)
                if prevk is not None and prevk in yst_done:
                    fw.append(yst_done[prevk])
                p.emit("vector", lambda e, o=o, tok=tok, tb=tb: e.tensor_mul(
                    yout[tb], dop[:, tok, :], dsb_s[:, o * 512:(o + 1) * 512]),
                    waits=fw)
                p.emit("vector", lambda e: e.drain())
                p.emit("vector", lambda e, o=o, tok=tok, tb=tb: e.tensor_add(
                    yout[tb], yout[tb], attn_proj[:, tok, o * 512:(o + 1) * 512]))
                p.emit("vector", lambda e: e.drain())
                p.emit("vector", lambda e, o=o, tok=tok, tb=tb, ob=ob: e.tensor_add(
                    yout[tb], yout[tb], xsl[ob][:, tok, :]), inc=sDVE)
                fin_done[(o, tok)] = sDVE.n
                p.emit("gpsimd", lambda e, o=o, tok=tok, tb=tb: e.dma_start(
                    out=y_out.rearrange("(t p) d -> p t d", p=128)[:, tok, o * 512:(o + 1) * 512],
                    in_=yout[tb]),
                    waits=[(sDVE, fin_done[(o, tok)])], inc=sST[tb], inc_n=16)
                yst_done[(o, tok)] = (sST[tb], sST[tb].n)
    p.emit("gpsimd", lambda e: e.nop(),
           waits=[(sST[0], sST[0].n), (sST[1], sST[1].n), (sSTB, sSTB.n)])

    with nc.Block() as block:
        @block.sync
        def _(eng):
            for f in p.ops["sync"]:
                f(eng)

        @block.tensor
        def _(eng):
            for f in p.ops["tensor"]:
                f(eng)

        @block.scalar
        def _(eng):
            for f in p.ops["scalar"]:
                f(eng)

        @block.vector
        def _(eng):
            for f in p.ops["vector"]:
                f(eng)

        @block.gpsimd
        def _(eng):
            for f in p.ops["gpsimd"]:
                f(eng)

    return nc


def _prev_h(qt, g, hl, back):
    idx = (qt * KVH + g) * 4 + hl - back
    if idx < 0:
        return None
    q2, rem = divmod(idx, KVH * 4)
    g2, hl2 = divmod(rem, 4)
    return (q2, g2, hl2)


def _host_prep(L0_attn_hidden_states, L0_attn_Wq, L0_attn_Wk, L0_attn_Wv,
               L0_attn_Wo, L0_ffn_gate_w, L0_ffn_gate_s, L0_ffn_up_w,
               L0_ffn_up_s, L0_ffn_down_w, L0_ffn_down_s):
    x = np.ascontiguousarray(np.asarray(L0_attn_hidden_states, dtype=np.float32)[0])
    Wq = np.asarray(L0_attn_Wq, dtype=np.float32)
    Wk = np.asarray(L0_attn_Wk, dtype=np.float32)
    Wv = np.asarray(L0_attn_Wv, dtype=np.float32)
    Wo = np.asarray(L0_attn_Wo, dtype=np.float32)
    gw = np.asarray(L0_ffn_gate_w, dtype=np.int32)
    uw = np.asarray(L0_ffn_up_w, dtype=np.int32)
    dw = np.asarray(L0_ffn_down_w, dtype=np.int32)
    gs = np.asarray(L0_ffn_gate_s, dtype=np.float32)
    us = np.asarray(L0_ffn_up_s, dtype=np.float32)
    ds = np.asarray(L0_ffn_down_s, dtype=np.float32)

    wq_b = np.ascontiguousarray(Wq.T.reshape(E, HQ, 128).transpose(1, 0, 2))
    wk_b = np.ascontiguousarray(Wk.T.reshape(E, KVH, 128).transpose(1, 0, 2))
    wv_t = np.ascontiguousarray(Wv.T)
    wo_b = np.ascontiguousarray(Wo.T.reshape(E, 8, 512).transpose(1, 0, 2))
    gw_b = np.ascontiguousarray(gw.T.reshape(E, FC, 128).transpose(1, 0, 2))
    uw_b = np.ascontiguousarray(uw.T.reshape(E, FC, 128).transpose(1, 0, 2))
    dw_b = np.ascontiguousarray(dw.T.reshape(FF, 8, 512).transpose(1, 0, 2))
    gs_t = np.ascontiguousarray(gs.reshape(FC, 128).T)
    us_t = np.ascontiguousarray(us.reshape(FC, 128).T)
    dsb = np.ascontiguousarray(np.broadcast_to(ds[None, :], (128, E)))

    inv_freq = (1.0 / (10000.0 ** (np.arange(0, HD, 2, dtype=np.float32) / HD)))
    t = np.arange(S, dtype=np.float32)
    freqs = t[:, None] * inv_freq[None, :]
    cos_fm = np.ascontiguousarray(np.cos(freqs).T.astype(np.float32))
    sin_fm = np.ascontiguousarray(np.sin(freqs).T.astype(np.float32))

    kpos = np.arange(S).reshape(NKT, 128)
    in_maps = []
    for c in range(NCORE):
        m = np.empty((2, NKT, 128, 512), dtype=ml_dtypes.bfloat16)
        for qt in range(2):
            qp = TPC * c + 512 * qt + np.arange(512)
            valid = (np.abs(kpos[:, :, None] - qp[None, None, :]) >= WIN)
            m[qt] = valid.astype(np.float32)
        in_maps.append({
            "x": np.ascontiguousarray(x[TPC * c:TPC * (c + 1)]),
            "wq": wq_b, "wk": wk_b, "wv": wv_t, "wo": wo_b,
            "gw": gw_b, "uw": uw_b, "dw": dw_b,
            "gs": gs_t, "us": us_t, "dsb": dsb,
            "cos64": np.ascontiguousarray(cos_fm[:, TPC * c:TPC * (c + 1)]),
            "sin64": np.ascontiguousarray(sin_fm[:, TPC * c:TPC * (c + 1)]),
            "masks": m,
        })
    return in_maps


def kernel(**inputs):
    in_maps = _host_prep(**inputs)
    nc = build_nc()
    res = run_bass_kernel_spmd(nc, in_maps, core_ids=list(range(NCORE)))
    y = np.concatenate([res.results[c]["y"] for c in range(NCORE)], axis=0)
    return y.reshape(1, S, E)


# version 05
# speedup vs baseline: 1.4988x; 1.4988x over previous
import math
from contextlib import ExitStack

import numpy as np
import ml_dtypes

import concourse.bass as bass
import concourse.mybir as mybir
from concourse.bass_utils import run_bass_kernel_spmd

F32 = mybir.dt.float32
BF16 = mybir.dt.bfloat16
I32 = mybir.dt.int32
AF = mybir.ActivationFunctionType
ALU = mybir.AluOpType

S, E = 8192, 4096
HQ, KVH, HD = 32, 8, 128
WIN = 1024
FF = 11008
NCORE = 8
TPC = S // NCORE
NT = TPC // 128
DC = E // 128
FC = FF // 128
NKT = S // 128
CSHIFT = 60.0
ISQ = 1.0 / math.sqrt(HD)
EPS = 1e-5


class Sem:
    def __init__(self, handle):
        self.h = handle
        self.n = 0


class Prog:

    ENGS = ("tensor", "scalar", "vector", "gpsimd", "sync")

    def __init__(self):
        self.ops = {e: [] for e in self.ENGS}
        self.waited = {e: {} for e in self.ENGS}

    def emit(self, eng, fn, waits=(), inc=None, inc_n=1):
        real_waits = []
        for (s, v) in waits:
            if v <= 0:
                continue
            prev = self.waited[eng].get(id(s), -1)
            if v > prev:
                real_waits.append((s, v))
                self.waited[eng][id(s)] = v
        if inc is not None:
            inc.n += inc_n

        def run(engine, fn=fn, real_waits=tuple(real_waits), inc=inc, inc_n=inc_n):
            for (s, v) in real_waits:
                engine.wait_ge(s.h, v)
            instr = fn(engine)
            if inc is not None:
                instr.then_inc(inc.h, inc_n)

        self.ops[eng].append(run)


def build_nc():
    nc = bass.Bass("TRN2", target_bir_lowering=False)

    x_in = nc.dram_tensor("x", [TPC, E], F32, kind="ExternalInput").ap()
    wq_in = nc.dram_tensor("wq", [HQ, E, 128], F32, kind="ExternalInput").ap()
    wk_in = nc.dram_tensor("wk", [KVH, E, 128], F32, kind="ExternalInput").ap()
    wv_in = nc.dram_tensor("wv", [E, KVH * 128], F32, kind="ExternalInput").ap()
    wo_in = nc.dram_tensor("wo", [8, E, 512], F32, kind="ExternalInput").ap()
    gw_in = nc.dram_tensor("gw", [FC, E, 128], I32, kind="ExternalInput").ap()
    uw_in = nc.dram_tensor("uw", [FC, E, 128], I32, kind="ExternalInput").ap()
    dw_in = nc.dram_tensor("dw", [8, FF, 512], I32, kind="ExternalInput").ap()
    gs_in = nc.dram_tensor("gs", [128, FC], F32, kind="ExternalInput").ap()
    us_in = nc.dram_tensor("us", [128, FC], F32, kind="ExternalInput").ap()
    dsb_in = nc.dram_tensor("dsb", [128, E], F32, kind="ExternalInput").ap()
    cos_in = nc.dram_tensor("cos64", [64, TPC], F32, kind="ExternalInput").ap()
    sin_in = nc.dram_tensor("sin64", [64, TPC], F32, kind="ExternalInput").ap()
    msk_in = nc.dram_tensor("masks", [2, NKT, 128, 512], BF16, kind="ExternalInput").ap()
    y_out = nc.dram_tensor("y", [TPC, E], F32, kind="ExternalOutput").ap()

    kself = nc.dram_tensor("kself", [KVH * 128, TPC], BF16).ap()
    kg = nc.dram_tensor("kg", [NCORE * KVH * 128, TPC], BF16).ap()
    vself = nc.dram_tensor("vself", [NT, KVH, 128, 128], BF16).ap()
    vg = nc.dram_tensor("vg", [NCORE, NT, KVH, 128, 128], BF16).ap()
    qbuf = nc.dram_tensor("qbuf", [HQ, 128, TPC], BF16).ap()
    abuf = nc.dram_tensor("abuf", [HQ, 128, TPC], BF16).ap()
    hbuf = nc.dram_tensor("hbuf", [FC, 128, TPC], BF16).ap()

    cos_s = nc.alloc_sbuf_tensor("cos_s", [64, TPC], F32).ap()
    sin_s = nc.alloc_sbuf_tensor("sin_s", [64, TPC], F32).ap()
    gs_s = nc.alloc_sbuf_tensor("gs_s", [128, FC], F32).ap()
    us_s = nc.alloc_sbuf_tensor("us_s", [128, FC], F32).ap()
    dsb_s = nc.alloc_sbuf_tensor("dsb_s", [128, E], F32).ap()
    ident = nc.alloc_sbuf_tensor("ident", [128, 128], BF16).ap()
    ones_col = nc.alloc_sbuf_tensor("ones_col", [128, 1], BF16).ap()
    ones_row = nc.alloc_sbuf_tensor("ones_row", [1, 128], F32).ap()
    c_scale = nc.alloc_sbuf_tensor("c_scale", [128, 1], F32).ap()
    c_bias = nc.alloc_sbuf_tensor("c_bias", [128, 1], F32).ap()
    c_eps = nc.alloc_sbuf_tensor("c_eps", [128, 1], F32).ap()
    ln_xsum = nc.alloc_sbuf_tensor("ln_xsum", [128, 2], F32).ap()
    ln_ssum = nc.alloc_sbuf_tensor("ln_ssum", [128, 2], F32).ap()
    ln_mean = nc.alloc_sbuf_tensor("ln_mean", [128, 2], F32).ap()
    ln_var = nc.alloc_sbuf_tensor("ln_var", [128, 2], F32).ap()
    ln_std = nc.alloc_sbuf_tensor("ln_std", [128, 2], F32).ap()
    ln_rstd = nc.alloc_sbuf_tensor("ln_rstd", [128, 2], F32).ap()
    ln_nbias = nc.alloc_sbuf_tensor("ln_nbias", [128, 2], F32).ap()
    ln_t = nc.alloc_sbuf_tensor("ln_t", [128, 2], F32).ap()

    BASE = (nc.sbuf_base + 31) & ~31
    TOP = nc.sbuf_top
    avail = TOP - BASE
    assert avail >= 180 * 1024, f"overlay too small: {avail}"

    made = {}

    def at(name, off_bytes, shape, dtype):
        key = (name, off_bytes)
        if key not in made:
            nbytes = int(np.prod(shape[1:])) * mybir.dt.size(dtype)
            assert off_bytes % 32 == 0
            assert off_bytes + nbytes <= avail, (name, off_bytes, nbytes, avail)
            made[key] = nc.alloc_sbuf_tensor_at(
                f"{name}_{off_bytes}", shape, dtype, offset=BASE + off_bytes).ap()
        return made[key]

    KB = 1024

    p = Prog()

    def mksem(name):
        return Sem(nc.semaphore(name).__enter__())

    sCC = mksem("sCC")
    sPE = mksem("sPE")
    sACT = mksem("sACT")
    sDVE = mksem("sDVE")
    sPOOL = mksem("sPOOL")
    sConst = mksem("sConst")
    sXT = [mksem("sXT0"), mksem("sXT1")]
    sWS = [mksem("sWS0"), mksem("sWS1")]
    sGS = mksem("sGS")
    sUS = mksem("sUS")
    sKSL = [mksem("sKSL0"), mksem("sKSL1")]
    sVSL = [mksem("sVSL0"), mksem("sVSL1")]
    sQSB = [mksem("sQSB0"), mksem("sQSB1")]
    sMSK = mksem("sMSK")
    sST = [mksem("sST0"), mksem("sST1")]
    sSTB = mksem("sSTB")
    ALL_SEMS = (sCC, sPE, sACT, sDVE, sPOOL, sConst, sXT[0], sXT[1],
                sWS[0], sWS[1], sGS, sUS, sKSL[0], sKSL[1], sVSL[0], sVSL[1],
                sQSB[0], sQSB[1], sMSK, sST[0], sST[1], sSTB)

    def snap():
        return [(s, s.n) for s in ALL_SEMS]

    p.emit("sync", lambda e: e.dma_start(out=cos_s, in_=cos_in), inc=sConst, inc_n=16)
    p.emit("sync", lambda e: e.dma_start(out=sin_s, in_=sin_in), inc=sConst, inc_n=16)
    p.emit("sync", lambda e: e.dma_start(out=gs_s, in_=gs_in), inc=sConst, inc_n=16)
    p.emit("sync", lambda e: e.dma_start(out=us_s, in_=us_in), inc=sConst, inc_n=16)
    p.emit("sync", lambda e: e.dma_start(out=dsb_s, in_=dsb_in), inc=sConst, inc_n=16)
    p.emit("vector", lambda e: e.memset(ident, 0.0))
    p.emit("vector", lambda e: e.memset(ones_col, 1.0))
    p.emit("vector", lambda e: e.memset(ones_row, 1.0))
    p.emit("vector", lambda e: e.memset(c_scale, ISQ))
    p.emit("vector", lambda e: e.memset(c_bias, -CSHIFT))
    p.emit("vector", lambda e: e.memset(c_eps, EPS), inc=sDVE)
    dve_const_done = sDVE.n
    p.emit("gpsimd", lambda e: e.affine_select(
        out=ident, in_=ident, compare_op=ALU.not_equal, fill=1.0,
        base=0, pattern=[[-1, 128]], channel_multiplier=1),
        waits=[(sDVE, dve_const_done)], inc=sPOOL)
    ident_done = sPOOL.n

    xtile = [at("xtile", i * 16 * KB, [128, E], F32) for i in range(2)]
    xsq = at("xsq", 32 * KB, [128, E], F32)
    xn_tm = [at("xn_tm", 48 * KB + i * 8 * KB, [128, E], BF16) for i in range(2)]
    xn_fm = at("xn_fm", 64 * KB, [128, DC, TPC], BF16)
    XN_OFF, XN_END = 64 * KB, 128 * KB

    with nc.psum_tensor("trp0", [128, 128], BF16) as trp0_t, \
         nc.psum_tensor("trp1", [128, 128], BF16) as trp1_t:
        trp = [trp0_t.ap(), trp1_t.ap()]
        xload_done = [0] * NT
        xn_done = [0] * NT
        red_done = [0] * NT
        tr_done = [0] * NT
        ev_done = [0] * (NT * DC)
        for it in range(NT):
            b = it % 2
            waits = []
            if it >= 2:
                waits = [(sDVE, xn_done[it - 2]), (sDVE, red_done[it - 2])]
            p.emit("sync", lambda e, b=b, it=it: e.dma_start(
                out=xtile[b], in_=x_in[it * 128:(it + 1) * 128, :]),
                waits=waits, inc=sXT[b], inc_n=16)
            xload_done[it] = (sXT[b], sXT[b].n)
            p.emit("scalar", lambda e, b=b: e.activation(
                out=xsq, in_=xtile[b], func=AF.Square),
                waits=[xload_done[it]] + ([(sDVE, red_done[it - 1])] if it >= 1 else []),
                inc=sACT)
            sq_done = sACT.n
            p.emit("vector", lambda e, b=b: e.reduce_sum(
                ln_xsum[:, b:b + 1], xtile[b], axis=mybir.AxisListType.X),
                waits=[xload_done[it]])
            p.emit("vector", lambda e, b=b: e.reduce_sum(
                ln_ssum[:, b:b + 1], xsq, axis=mybir.AxisListType.X),
                waits=[(sACT, sq_done)], inc=sDVE)
            red_done[it] = sDVE.n
            p.emit("vector", lambda e: e.drain())
            p.emit("vector", lambda e, b=b: e.tensor_scalar_mul(
                ln_mean[:, b:b + 1], ln_xsum[:, b:b + 1], 1.0 / E))
            p.emit("vector", lambda e, b=b: e.tensor_scalar_mul(
                ln_t[:, b:b + 1], ln_ssum[:, b:b + 1], 1.0 / E))
            p.emit("vector", lambda e: e.drain())
            p.emit("vector", lambda e, b=b: e.scalar_tensor_tensor(
                out=ln_var[:, b:b + 1], in0=ln_mean[:, b:b + 1], scalar=-1.0,
                in1=ln_mean[:, b:b + 1], op0=ALU.mult, op1=ALU.mult))
            p.emit("vector", lambda e: e.drain())
            p.emit("vector", lambda e, b=b: e.tensor_add(
                ln_var[:, b:b + 1], ln_var[:, b:b + 1], ln_t[:, b:b + 1]), inc=sDVE)
            var_done = sDVE.n
            p.emit("scalar", lambda e, b=b: e.activation(
                out=ln_std[:, b:b + 1], in_=ln_var[:, b:b + 1], func=AF.Sqrt,
                bias=c_eps[:, 0:1], scale=1.0),
                waits=[(sDVE, var_done)], inc=sACT)
            std_done = sACT.n
            p.emit("vector", lambda e, b=b: e.reciprocal(
                ln_rstd[:, b:b + 1], ln_std[:, b:b + 1]),
                waits=[(sACT, std_done)])
            p.emit("vector", lambda e: e.drain())
            p.emit("vector", lambda e, b=b: e.scalar_tensor_tensor(
                out=ln_nbias[:, b:b + 1], in0=ln_mean[:, b:b + 1], scalar=-1.0,
                in1=ln_rstd[:, b:b + 1], op0=ALU.mult, op1=ALU.mult), inc=sDVE)
            nb_done = sDVE.n
            p.emit("vector", lambda e: e.drain())
            xn_waits = [(sDVE, nb_done)]
            if it >= 2:
                xn_waits.append((sPE, tr_done[it - 2]))
            p.emit("vector", lambda e, b=b: e.tensor_scalar(
                out=xn_tm[b], in0=xtile[b],
                scalar1=ln_rstd[:, b:b + 1], scalar2=ln_nbias[:, b:b + 1],
                op0=ALU.mult, op1=ALU.add),
                waits=xn_waits, inc=sDVE)
            xn_done[it] = sDVE.n
            for d in range(DC):
                tw = [(sDVE, xn_done[it]), (sPOOL, ident_done)]
                prev = it * DC + d - 2
                if prev >= 0:
                    tw.append((sDVE, ev_done[prev]))
                p.emit("tensor", lambda e, b=b, d=d: e.transpose(
                    trp[d % 2], xn_tm[b][:, d * 128:(d + 1) * 128], ident),
                    waits=tw, inc=sPE)
                tr_done[it] = sPE.n
                p.emit("vector", lambda e, it=it, d=d: e.tensor_copy(
                    out=xn_fm[:, d, it * 128:(it + 1) * 128], in_=trp[d % 2]),
                    waits=[(sPE, sPE.n)], inc=sDVE)
                ev_done[it * DC + d] = sDVE.n
        xn_ready = sDVE.n

    V_tm = at("V_tm", 0, [128, NT, TPC], BF16)
    K_fm = at("K_fm", 16 * KB, [128, KVH, TPC], BF16)
    wstage = [at("wstage", 32 * KB + i * 16 * KB, [128, 8, 512], F32) for i in range(2)]
    wv_bf = at("wv_bf", 128 * KB, [128, DC, 512], BF16)
    wkstage = [at("wkstage", 32 * KB + i * 16 * KB, [128, DC, 128], F32) for i in range(2)]
    wk_bf = [at("wk_bf", 128 * KB + i * 8 * KB, [128, DC, 128], BF16) for i in range(2)]
    rsc = [at("rsc", 160 * KB + i * 2 * KB, [64, 512], F32) for i in range(4)]

    s2bar = snap()
    with nc.psum_tensor("pv", [128, 2, 512], F32) as pv_t, \
         nc.psum_tensor("pk", [128, 2, 2, 512], F32) as pk_t:
        pv = pv_t.ap()
        pk = pk_t.ap()
        vconv_done = [0] * 8
        vmm_done = [0] * (2 * NT)
        vevac_done = [0] * (2 * NT)
        for vc in range(2):
            for dg in range(4):
                b = dg % 2
                w = list(s2bar) if (vc == 0 and dg < 2) else []
                if vc * 4 + dg >= 2:
                    w.append((sACT, vconv_done[vc * 4 + dg - 2]))
                p.emit("sync", lambda e, b=b, vc=vc, dg=dg: e.dma_start(
                    out=wstage[b],
                    in_=wv_in.rearrange("(c p) n -> p c n", p=128)[:, dg * 8:(dg + 1) * 8, vc * 512:(vc + 1) * 512]),
                    waits=w, inc=sWS[b], inc_n=16)
                p.emit("scalar", lambda e, b=b, dg=dg: e.activation(
                    out=wv_bf[:, dg * 8:(dg + 1) * 8, :], in_=wstage[b], func=AF.Copy),
                    waits=[(sWS[b], sWS[b].n)] + (list(s2bar) if (vc == 0 and dg == 0) else []),
                    inc=sACT)
                vconv_done[vc * 4 + dg] = sACT.n
            for it in range(NT):
                b = it % 2
                idx = vc * NT + it
                mw = [(sDVE, xn_ready), (sACT, vconv_done[vc * 4 + 3])]
                if idx >= 2:
                    mw.append((sACT, vevac_done[idx - 2]))
                for d in range(DC):
                    p.emit("tensor", lambda e, b=b, it=it, d=d: e.matmul(
                        pv[:, b, :], xn_fm[:, d, it * 128:(it + 1) * 128],
                        wv_bf[:, d, :], start=(d == 0), stop=(d == DC - 1)),
                        waits=mw if d == 0 else (), inc=(sPE if d == DC - 1 else None))
                vmm_done[idx] = sPE.n
                p.emit("scalar", lambda e, b=b, it=it, vc=vc: e.activation(
                    out=V_tm[:, it, vc * 512:(vc + 1) * 512], in_=pv[:, b, :], func=AF.Copy),
                    waits=[(sPE, vmm_done[idx])], inc=sACT)
                vevac_done[idx] = sACT.n
        for kt2 in range(NT):
            p.emit("gpsimd", lambda e, kt2=kt2: e.dma_start(
                out=vself[kt2].rearrange("g p v -> p g v"),
                in_=V_tm[:, kt2, :].rearrange("p (g v) -> p g v", g=KVH)),
                waits=[(sACT, vevac_done[NT + kt2])], inc=sSTB, inc_n=16)
        vstore_done = sSTB.n
        kconv_done = [0] * KVH
        kmm_done = [0] * KVH
        krope_done = [0] * KVH
        for kc in range(KVH):
            b = kc % 2
            w = [(sACT, vconv_done[7])] if kc < 2 else [(sACT, kconv_done[kc - 2])]
            p.emit("sync", lambda e, b=b, kc=kc: e.dma_start(
                out=wkstage[b], in_=wk_in[kc].rearrange("(c p) m -> p c m", p=128)),
                waits=w, inc=sWS[b], inc_n=16)
            cw = [(sWS[b], sWS[b].n)]
            if kc >= 2:
                cw.append((sPE, kmm_done[kc - 2]))
            p.emit("scalar", lambda e, b=b: e.activation(
                out=wk_bf[b], in_=wkstage[b], func=AF.Copy), waits=cw, inc=sACT)
            kconv_done[kc] = sACT.n
            mw = [(sACT, kconv_done[kc]), (sDVE, xn_ready)]
            if kc >= 2:
                mw.append((sDVE, krope_done[kc - 2]))
            for tt in range(2):
                for d in range(DC):
                    p.emit("tensor", lambda e, b=b, tt=tt, d=d: e.matmul(
                        pk[:, b, tt, :], wk_bf[b][:, d, :],
                        xn_fm[:, d, tt * 512:(tt + 1) * 512],
                        start=(d == 0), stop=(d == DC - 1)),
                        waits=mw if (tt == 0 and d == 0) else (),
                        inc=(sPE if (tt == 1 and d == DC - 1) else None))
            kmm_done[kc] = sPE.n
            for tt in range(2):
                cosl = cos_s[:, tt * 512:(tt + 1) * 512]
                sinl = sin_s[:, tt * 512:(tt + 1) * 512]
                sl = slice(tt * 512, (tt + 1) * 512)
                p.emit("vector", lambda e, b=b, tt=tt, cosl=cosl: e.tensor_mul(
                    rsc[0], pk[0:64, b, tt, :], cosl),
                    waits=[(sPE, kmm_done[kc]), (sConst, 5 * 16)])
                p.emit("vector", lambda e, b=b, tt=tt, sinl=sinl: e.tensor_mul(
                    rsc[1], pk[64:128, b, tt, :], sinl))
                p.emit("vector", lambda e: e.drain())
                p.emit("vector", lambda e, kc=kc, sl=sl: e.tensor_sub(
                    K_fm[0:64, kc, sl], rsc[0], rsc[1]))
                p.emit("vector", lambda e, b=b, tt=tt, cosl=cosl: e.tensor_mul(
                    rsc[2], pk[64:128, b, tt, :], cosl))
                p.emit("vector", lambda e, b=b, tt=tt, sinl=sinl: e.tensor_mul(
                    rsc[3], pk[0:64, b, tt, :], sinl))
                p.emit("vector", lambda e: e.drain())
                p.emit("vector", lambda e, kc=kc, sl=sl: e.tensor_add(
                    K_fm[64:128, kc, sl], rsc[2], rsc[3]),
                    inc=(sDVE if tt == 1 else None))
            krope_done[kc] = sDVE.n
        p.emit("gpsimd", lambda e: e.dma_start(
            out=kself.rearrange("(g p) t -> p g t", p=128), in_=K_fm),
            waits=[(sDVE, krope_done[KVH - 1])], inc=sSTB, inc_n=16)
        kvstore_total = sSTB.n
        p.emit("gpsimd", lambda e: e.collective_compute(
            "AllGather", ALU.bypass,
            replica_groups=[list(range(NCORE))],
            ins=[kself.opt()], outs=[kg.opt()]),
            waits=[(sSTB, kvstore_total)], inc=sCC)
        p.emit("gpsimd", lambda e: e.collective_compute(
            "AllGather", ALU.bypass,
            replica_groups=[list(range(NCORE))],
            ins=[vself.rearrange("a g p v -> (a g p) v").opt()],
            outs=[vg.rearrange("r a g p v -> (r a g p) v").opt()]),
            waits=[(sSTB, kvstore_total)], inc=sCC)
        cc_done = sCC.n

    qchunk = [at("qchunk", 176 * KB + i * 2 * KB, [128, TPC], BF16) for i in range(2)]
    s3bar = snap()
    with nc.psum_tensor("pq", [128, 2, 2, 512], F32) as pq_t:
        pq = pq_t.ap()
        qconv_done = [0] * HQ
        qmm_done = [0] * HQ
        qst_done = [0] * HQ
        qrope_done = [0] * HQ
        for qc in range(HQ):
            b = qc % 2
            w = [(sACT, qconv_done[qc - 2])] if qc >= 2 else list(s3bar)
            p.emit("sync", lambda e, b=b, qc=qc: e.dma_start(
                out=wkstage[b], in_=wq_in[qc].rearrange("(c p) m -> p c m", p=128)),
                waits=w, inc=sWS[b], inc_n=16)
            cw = [(sWS[b], sWS[b].n)]
            if qc >= 2:
                cw.append((sPE, qmm_done[qc - 2]))
            else:
                cw += list(s3bar)
            p.emit("scalar", lambda e, b=b: e.activation(
                out=wk_bf[b], in_=wkstage[b], func=AF.Copy), waits=cw, inc=sACT)
            qconv_done[qc] = sACT.n
            mw = [(sACT, qconv_done[qc])]
            if qc >= 2:
                mw.append((sDVE, qrope_done[qc - 2]))
            for tt in range(2):
                for d in range(DC):
                    p.emit("tensor", lambda e, b=b, tt=tt, d=d: e.matmul(
                        pq[:, b, tt, :], wk_bf[b][:, d, :],
                        xn_fm[:, d, tt * 512:(tt + 1) * 512],
                        start=(d == 0), stop=(d == DC - 1)),
                        waits=mw if (tt == 0 and d == 0) else (),
                        inc=(sPE if (tt == 1 and d == DC - 1) else None))
            qmm_done[qc] = sPE.n
            rw = [(sPE, qmm_done[qc])]
            if qc >= 2:
                rw.append(qst_done[qc - 2])
            for tt in range(2):
                cosl = cos_s[:, tt * 512:(tt + 1) * 512]
                sinl = sin_s[:, tt * 512:(tt + 1) * 512]
                sl = slice(tt * 512, (tt + 1) * 512)
                p.emit("vector", lambda e, b=b, tt=tt, cosl=cosl: e.tensor_mul(
                    rsc[0], pq[0:64, b, tt, :], cosl), waits=rw if tt == 0 else ())
                p.emit("vector", lambda e, b=b, tt=tt, sinl=sinl: e.tensor_mul(
                    rsc[1], pq[64:128, b, tt, :], sinl))
                p.emit("vector", lambda e: e.drain())
                p.emit("vector", lambda e, b=b, sl=sl: e.tensor_sub(
                    qchunk[b][0:64, sl], rsc[0], rsc[1]))
                p.emit("vector", lambda e, b=b, tt=tt, cosl=cosl: e.tensor_mul(
                    rsc[2], pq[64:128, b, tt, :], cosl))
                p.emit("vector", lambda e, b=b, tt=tt, sinl=sinl: e.tensor_mul(
                    rsc[3], pq[0:64, b, tt, :], sinl))
                p.emit("vector", lambda e: e.drain())
                p.emit("vector", lambda e, b=b, sl=sl: e.tensor_add(
                    qchunk[b][64:128, sl], rsc[2], rsc[3]),
                    inc=(sDVE if tt == 1 else None))
            qrope_done[qc] = sDVE.n
            p.emit("gpsimd", lambda e, b=b, qc=qc: e.dma_start(
                out=qbuf[qc], in_=qchunk[b]),
                waits=[(sDVE, qrope_done[qc])], inc=sST[b], inc_n=16)
            qst_done[qc] = (sST[b], sST[b].n)
        qspill_done = [(sST[0], sST[0].n), (sST[1], sST[1].n)]

    gstage = at("gstage", 0, [128, DC, 128], I32)
    ustage = at("ustage", 16 * KB, [128, DC, 128], I32)
    g_bf = [at("g_bf", 32 * KB + i * 8 * KB, [128, DC, 128], BF16) for i in range(2)]
    u_bf = [at("u_bf", 48 * KB + i * 8 * KB, [128, DC, 128], BF16) for i in range(2)]
    tg_s = at("tg_s", 168 * KB, [128, TPC], BF16)
    tu_s = at("tu_s", 172 * KB, [128, TPC], BF16)
    h_t = [at("h_t", 176 * KB + i * 2 * KB, [128, TPC], BF16) for i in range(2)]
    s4bar = snap()
    with nc.psum_tensor("pg", [128, 2, TPC], F32) as pg_t, \
         nc.psum_tensor("pu", [128, 2, TPC], F32) as pu_t:
        pg = pg_t.ap()
        pu = pu_t.ap()
        gconv_done = [0] * FC
        uconv_done = [0] * FC
        fmm_done = [0] * FC
        hmul_done = [0] * FC
        hst_done = [0] * FC
        for f in range(FC):
            b = f % 2
            lw = [(sDVE, gconv_done[f - 1])] if f >= 1 else list(s4bar)
            p.emit("sync", lambda e, f=f: e.dma_start(
                out=gstage, in_=gw_in[f].rearrange("(c p) m -> p c m", p=128)),
                waits=lw, inc=sGS, inc_n=16)
            gload = (sGS, sGS.n)
            uw_w = [(sACT, uconv_done[f - 1])] if f >= 1 else list(s4bar)
            p.emit("sync", lambda e, f=f: e.dma_start(
                out=ustage, in_=uw_in[f].rearrange("(c p) m -> p c m", p=128)),
                waits=uw_w, inc=sUS, inc_n=16)
            uload = (sUS, sUS.n)
            gw_w = [gload]
            uw_w2 = [uload]
            if f >= 2:
                gw_w.append((sPE, fmm_done[f - 2]))
                uw_w2.append((sPE, fmm_done[f - 2]))
            if f < 2:
                gw_w += list(s4bar)
                uw_w2 += list(s4bar)
            p.emit("vector", lambda e, b=b: e.tensor_copy(out=g_bf[b], in_=gstage),
                   waits=gw_w, inc=sDVE)
            gconv_done[f] = sDVE.n
            p.emit("scalar", lambda e, b=b: e.activation(
                out=u_bf[b], in_=ustage, func=AF.Copy), waits=uw_w2, inc=sACT)
            uconv_done[f] = sACT.n
            mw = [(sDVE, gconv_done[f]), (sACT, uconv_done[f])]
            if f >= 2:
                mw.append((sDVE, hmul_done[f - 2]))
            for d in range(DC):
                for tt in range(2):
                    p.emit("tensor", lambda e, b=b, d=d, tt=tt: e.matmul(
                        pg[:, b, tt * 512:(tt + 1) * 512], g_bf[b][:, d, :],
                        xn_fm[:, d, tt * 512:(tt + 1) * 512],
                        start=(d == 0), stop=(d == DC - 1), skip_group_check=True),
                        waits=mw if (d == 0 and tt == 0) else ())
            for d in range(DC):
                for tt in range(2):
                    p.emit("tensor", lambda e, b=b, d=d, tt=tt: e.matmul(
                        pu[:, b, tt * 512:(tt + 1) * 512], u_bf[b][:, d, :],
                        xn_fm[:, d, tt * 512:(tt + 1) * 512],
                        start=(d == 0), stop=(d == DC - 1), skip_group_check=True),
                        inc=(sPE if (d == DC - 1 and tt == 1) else None))
            fmm_done[f] = sPE.n
            ew = [(sPE, fmm_done[f])]
            if f >= 1:
                ew.append((sDVE, hmul_done[f - 1]))
            if f >= 2:
                ew.append(hst_done[f - 2])
            p.emit("scalar", lambda e, f=f, b=b: e.activation(
                out=tg_s, in_=pg[:, b], func=AF.Silu, scale=gs_s[:, f:f + 1]),
                waits=ew, inc=sACT)
            tg_done = sACT.n
            p.emit("vector", lambda e, b=b, f=f: e.scalar_tensor_tensor(
                out=h_t[b], in0=pu[:, b], scalar=us_s[:, f:f + 1], in1=tg_s,
                op0=ALU.mult, op1=ALU.mult),
                waits=[(sACT, tg_done), (sPE, fmm_done[f])], inc=sDVE)
            hmul_done[f] = sDVE.n
            p.emit("gpsimd", lambda e, b=b, f=f: e.dma_start(
                out=hbuf[f], in_=h_t[b]),
                waits=[(sDVE, hmul_done[f])], inc=sST[b], inc_n=16)
            hst_done[f] = (sST[b], sST[b].n)
        hall_done = [(sST[0], sST[0].n), (sST[1], sST[1].n)]

    masks_s = at("masks_s", 0, [128, NKT, 512], BF16)
    kslab = [at("kslab", 64 * KB + i * 16 * KB, [128, NCORE, TPC], BF16) for i in range(2)]
    vslab = [at("vslab", 96 * KB + i * 16 * KB, [128, NKT, 128], BF16) for i in range(2)]
    qsb = [at("qsb", 128 * KB + i * KB, [128, 512], BF16) for i in range(2)]
    ptb = [at("ptb", 130 * KB + i * 2 * KB, [128, 2, 512], BF16) for i in range(3)]
    bc_s = [at("bc_s", 136 * KB + i * 2 * KB, [128, 512], F32) for i in range(2)]
    rec_s = [at("rec_s", 140 * KB + i * 2 * KB, [1, 512], F32) for i in range(2)]
    aout = [at("aout", 144 * KB + i * KB, [128, 512], BF16) for i in range(2)]

    s5bar = snap()
    NP = NKT // 2
    LAG = 2
    with nc.psum_tensor("stp", [128, 3, 2, 512], F32) as stp_t, \
         nc.psum_tensor("avp", [128, 512], F32) as avp_t, \
         nc.psum_tensor("dnbc", [128, 512], F32) as dnbc_t:
        stp = stp_t.ap()
        avp = avp_t.ap()
        dnbc = dnbc_t.ap()
        dnp = dnbc[0:1, :]
        mask_ld = [0, 0]
        slab_ld = {}
        q_ld = {}
        score_done = {}
        hfin = {}
        prev_exp_par = None
        for qt in range(2):
            mw = list(s5bar) if qt == 0 else [(sPE, sPE.n)]
            for half in range(2):
                p.emit("gpsimd", lambda e, qt=qt, half=half: e.dma_start(
                    out=masks_s[:, half * 32:(half + 1) * 32, :],
                    in_=msk_in[qt].rearrange("k p q -> p k q")[:, half * 32:(half + 1) * 32, :]),
                    waits=mw if half == 0 else (), inc=sMSK, inc_n=16)
            mask_ld[qt] = (sMSK, sMSK.n)
            for g in range(KVH):
                b = g % 2
                sw = [(sCC, cc_done)]
                if qt == 0 and g < 2:
                    sw += list(s5bar)
                key_prev = (qt, g - 2) if g >= 2 else ((qt - 1, g + KVH - 2) if qt > 0 else None)
                if key_prev in score_done:
                    sw.append((sPE, score_done[key_prev]))
                p.emit("gpsimd", lambda e, g=g, b=b: e.dma_start(
                    out=kslab[b],
                    in_=kg.rearrange("(r gg p) t -> gg p r t", gg=KVH, p=128)[g]),
                    waits=sw, inc=sKSL[b], inc_n=16)
                for half in range(2):
                    p.emit("gpsimd", lambda e, g=g, b=b, half=half: e.dma_start(
                        out=vslab[b][:, half * 32:(half + 1) * 32, :],
                        in_=vg.rearrange("r a gg p v -> gg p (r a) v")[g][:, half * 32:(half + 1) * 32, :]),
                        inc=sVSL[b], inc_n=16)
                slab_ld[(qt, g)] = ((sKSL[b], sKSL[b].n), (sVSL[b], sVSL[b].n))
                for hl in range(4):
                    h = 4 * g + hl
                    hb = h % 2
                    qw = list(qspill_done)
                    hkey = (qt, g, hl)
                    pkey = _prev_h(qt, g, hl, 2)
                    if pkey in score_done:
                        qw.append((sPE, score_done[pkey]))
                    p.emit("gpsimd", lambda e, h=h, hb=hb, qt=qt: e.dma_start(
                        out=qsb[hb], in_=qbuf[h][:, qt * 512:(qt + 1) * 512]),
                        waits=qw, inc=sQSB[hb], inc_n=16)
                    q_ld[hkey] = (sQSB[hb], sQSB[hb].n)
                    p1 = _prev_h(qt, g, hl, 1)
                    exp_done = {}
                    mask_done = {}

                    def emit_sc(k, b=b, hb=hb, hkey=hkey, qt=qt, g=g,
                                exp_done=exp_done, prev_par=prev_exp_par):
                        par = k % 3
                        w = []
                        if k == 0:
                            w += [slab_ld[(qt, g)][0], slab_ld[(qt, g)][1], q_ld[hkey]]
                        if k >= 3:
                            w.append((sACT, exp_done[k - 3]))
                        elif prev_par is not None:
                            w.append((sACT, prev_par[par]))
                        for j in range(2):
                            kt = 2 * k + j
                            r, kt2 = kt // 8, kt % 8
                            p.emit("tensor",
                                   lambda e, par=par, j=j, r=r, kt2=kt2, hb=hb, b=b: e.matmul(
                                       stp[:, par, j, :],
                                       kslab[b][:, r, kt2 * 128:(kt2 + 1) * 128],
                                       qsb[hb], start=True, stop=True,
                                       skip_group_check=True),
                                   waits=w if j == 0 else (),
                                   inc=(sPE if j == 1 else None))
                        return sPE.n

                    def emit_av(k, b=b, mask_done=mask_done):
                        par = k % 3
                        w = [(sDVE, mask_done[k])]
                        if k == 0 and p1 in hfin:
                            w.append((sDVE, hfin[p1][0]))
                        for j in range(2):
                            kt = 2 * k + j
                            p.emit("tensor",
                                   lambda e, par=par, j=j, kt=kt, b=b: e.matmul(
                                       avp, vslab[b][:, kt, :], ptb[par][:, j, :],
                                       start=(kt == 0), stop=(kt == NKT - 1),
                                       skip_group_check=True),
                                   waits=w if j == 0 else ())
                            p.emit("tensor",
                                   lambda e, par=par, j=j, kt=kt: e.matmul(
                                       dnp, ones_col, ptb[par][:, j, :],
                                       start=(kt == 0), stop=(kt == NKT - 1),
                                       skip_group_check=True),
                                   inc=(sPE if j == 1 else None))

                    for k in range(NP):
                        par = k % 3
                        sc_pair = emit_sc(k)
                        p.emit("scalar", lambda e, par=par: e.activation(
                            out=ptb[par].rearrange("p a q -> p (a q)"),
                            in_=stp[:, par].rearrange("p a q -> p (a q)"),
                            func=AF.Exp, scale=c_scale[:, 0:1], bias=c_bias[:, 0:1]),
                            waits=[(sPE, sc_pair)], inc=sACT)
                        exp_done[k] = sACT.n
                        p.emit("vector", lambda e, par=par, k=k: e.tensor_mul(
                            ptb[par].rearrange("p a q -> p (a q)"),
                            ptb[par].rearrange("p a q -> p (a q)"),
                            masks_s[:, 2 * k:2 * k + 2, :].rearrange("p a q -> p (a q)")),
                            waits=[(sACT, exp_done[k]), mask_ld[qt]], inc=sDVE)
                        mask_done[k] = sDVE.n
                        if k >= LAG:
                            emit_av(k - LAG)
                    for k in range(NP - LAG, NP):
                        emit_av(k)
                    p.emit("vector", lambda e, hb=hb: e.reciprocal(rec_s[hb], dnp),
                           waits=[(sPE, sPE.n)], inc=sDVE)
                    rec_done = sDVE.n
                    p.emit("tensor", lambda e, hb=hb: e.matmul(
                        dnbc, ones_row, rec_s[hb], start=True, stop=True,
                        skip_group_check=True),
                        waits=[(sDVE, rec_done)], inc=sPE)
                    bc_mm = sPE.n
                    p.emit("scalar", lambda e, hb=hb: e.activation(
                        out=bc_s[hb], in_=dnbc, func=AF.Copy),
                        waits=[(sPE, bc_mm)], inc=sACT)
                    bc_ev = sACT.n
                    aow = [(sACT, bc_ev)]
                    if p1 in hfin:
                        aow.append(hfin[p1][1])
                    p.emit("vector", lambda e, hb=hb: e.tensor_mul(
                        aout[hb], avp, bc_s[hb]), waits=aow, inc=sDVE)
                    av_used = sDVE.n
                    p.emit("gpsimd", lambda e, h=h, hb=hb, qt=qt: e.dma_start(
                        out=abuf[h][:, qt * 512:(qt + 1) * 512], in_=aout[hb]),
                        waits=[(sDVE, av_used)], inc=sST[hb], inc_n=16)
                    hfin[hkey] = (av_used, (sST[hb], sST[hb].n))
                    score_done[hkey] = sPE.n
                    prev_exp_par = {kk % 3: exp_done[kk] for kk in range(NP - 3, NP)}
        attn_done = [(sST[0], sST[0].n), (sST[1], sST[1].n)]

    attn_proj = at("attn_proj", 0, [128, NT, E], BF16)
    aslab = [at("aslab", 64 * KB + i * 2 * KB, [128, TPC], BF16) for i in range(2)]
    wostage = [at("wostage", 68 * KB + i * 16 * KB, [128, 8, 512], F32) for i in range(2)]
    wo_bf = at("wo_bf", 100 * KB, [128, DC, 512], BF16)

    s6bar = snap()
    with nc.psum_tensor("wop", [128, NT, 512], F32) as wop_t:
        wop = wop_t.ap()
        woconv_done = [0] * (8 * 4)
        aslab_ld = [0] * (8 * DC)
        womm_done = [0] * 8
        woev_done = [0] * 8
        for e_blk in range(8):
            for dg in range(4):
                b = dg % 2
                idx = e_blk * 4 + dg
                w = [(sACT, woconv_done[idx - 2])] if idx >= 2 else list(s6bar)
                p.emit("sync", lambda e, e_blk=e_blk, dg=dg, b=b: e.dma_start(
                    out=wostage[b],
                    in_=wo_in[e_blk].rearrange("(c p) n -> p c n", p=128)[:, dg * 8:(dg + 1) * 8, :]),
                    waits=w, inc=sWS[b], inc_n=16)
                cw = [(sWS[b], sWS[b].n)]
                if e_blk > 0 and dg < 2:
                    cw.append((sPE, womm_done[e_blk - 1]))
                if e_blk == 0 and dg == 0:
                    cw += list(s6bar)
                p.emit("scalar", lambda e, dg=dg, b=b: e.activation(
                    out=wo_bf[:, dg * 8:(dg + 1) * 8, :], in_=wostage[b], func=AF.Copy),
                    waits=cw, inc=sACT)
                woconv_done[idx] = sACT.n
            for hd in range(DC):
                b = hd % 2
                aw = list(attn_done)
                if e_blk * DC + hd >= 2:
                    aw.append((sPE, aslab_ld[e_blk * DC + hd - 2]))
                if e_blk == 0 and hd < 2:
                    aw += list(s6bar)
                p.emit("gpsimd", lambda e, hd=hd, b=b: e.dma_start(
                    out=aslab[b], in_=abuf[hd]), waits=aw, inc=sVSL[b], inc_n=16)
                asl = (sVSL[b], sVSL[b].n)
                mw = [asl, (sACT, woconv_done[e_blk * 4 + hd // 8])]
                if hd == 0 and e_blk > 0:
                    mw.append((sACT, woev_done[e_blk - 1]))
                for tok in range(NT):
                    p.emit("tensor", lambda e, b=b, hd=hd, tok=tok: e.matmul(
                        wop[:, tok, :], aslab[b][:, tok * 128:(tok + 1) * 128],
                        wo_bf[:, hd, :], start=(hd == 0), stop=(hd == DC - 1),
                        skip_group_check=True),
                        waits=mw if tok == 0 else (),
                        inc=(sPE if tok == NT - 1 else None))
                aslab_ld[e_blk * DC + hd] = sPE.n
            womm_done[e_blk] = sPE.n
            for tok in range(NT):
                p.emit("scalar", lambda e, tok=tok, e_blk=e_blk: e.activation(
                    out=attn_proj[:, tok, e_blk * 512:(e_blk + 1) * 512],
                    in_=wop[:, tok, :], func=AF.Copy),
                    waits=[(sPE, womm_done[e_blk])],
                    inc=(sACT if tok == NT - 1 else None))
            woev_done[e_blk] = sACT.n

    hsl = [at("hsl", 64 * KB + i * 2 * KB, [128, TPC], BF16) for i in range(2)]
    dwstage = [at("dwstage", 68 * KB + i * 2 * KB, [128, 512], I32) for i in range(2)]
    dwb = [at("dwb", 72 * KB + i * KB, [128, 512], BF16) for i in range(2)]
    xsl = [at("xsl", 80 * KB + i * 16 * KB, [128, NT, 512], F32) for i in range(2)]
    yout = [at("yout", 112 * KB + i * 2 * KB, [128, 512], F32) for i in range(2)]
    t1b = [at("t1b", 116 * KB + i * 2 * KB, [128, 512], F32) for i in range(2)]

    s7bar = snap()
    with nc.psum_tensor("dop", [128, NT, 512], F32) as dop_t:
        dop = dop_t.ap()
        dconv_done = {}
        dmmrow_done = {}
        dmm_done = [0] * 8
        dvet1_done = [0] * 8
        fin_done = {}
        yst_done = {}
        for o in range(8):
            ob = o % 2
            xw = [(sPOOL, fin_done[(o - 2, NT - 1)][1])] if o >= 2 else list(s7bar)
            p.emit("sync", lambda e, o=o, ob=ob: e.dma_start(
                out=xsl[ob],
                in_=x_in.rearrange("(t p) d -> p t d", p=128)[:, :, o * 512:(o + 1) * 512]),
                waits=xw, inc=sXT[ob], inc_n=16)
            xsl_ld = (sXT[ob], sXT[ob].n)
            for f in range(FC):
                fb = f % 2
                idx = o * FC + f
                w = [(sACT, dconv_done[idx - 2])] if idx >= 2 else (list(s7bar) if o == 0 else [])
                dsem = sGS if fb == 0 else sUS
                p.emit("sync", lambda e, o=o, f=f, fb=fb: e.dma_start(
                    out=dwstage[fb], in_=dw_in[o][f * 128:(f + 1) * 128, :]),
                    waits=w, inc=dsem, inc_n=16)
                dld = (dsem, dsem.n)
                cw = [dld]
                if idx >= 2:
                    cw.append((sPE, dmmrow_done[idx - 2]))
                elif o == 0:
                    cw += list(s7bar)
                p.emit("scalar", lambda e, fb=fb: e.activation(
                    out=dwb[fb], in_=dwstage[fb], func=AF.Copy), waits=cw, inc=sACT)
                dconv_done[idx] = sACT.n
                hw = list(hall_done)
                if idx >= 2:
                    hw.append((sPE, dmmrow_done[idx - 2]))
                if o == 0 and f < 2:
                    hw += list(s7bar)
                p.emit("gpsimd", lambda e, f=f, fb=fb: e.dma_start(
                    out=hsl[fb], in_=hbuf[f]), waits=hw, inc=sQSB[fb], inc_n=16)
                hld = (sQSB[fb], sQSB[fb].n)
                mw = [(sACT, dconv_done[idx]), hld]
                if f == 0 and o > 0:
                    mw.append((sDVE, dvet1_done[o - 1]))
                for tok in range(NT):
                    p.emit("tensor", lambda e, fb=fb, tok=tok, f=f: e.matmul(
                        dop[:, tok, :], hsl[fb][:, tok * 128:(tok + 1) * 128],
                        dwb[fb], start=(f == 0), stop=(f == FC - 1),
                        skip_group_check=True),
                        waits=mw if tok == 0 else (),
                        inc=(sPE if tok == NT - 1 else None))
                dmmrow_done[idx] = sPE.n
            dmm_done[o] = sPE.n
            for tok in range(NT):
                tb = tok % 2
                fw = [(sPE, dmm_done[o])]
                if tok >= 2:
                    fw.append((sPOOL, fin_done[(o, tok - 2)][1]))
                elif o > 0:
                    fw.append((sPOOL, fin_done[(o - 1, NT + tok - 2)][1]))
                p.emit("vector", lambda e, o=o, tok=tok, tb=tb: e.tensor_mul(
                    t1b[tb], dop[:, tok, :], dsb_s[:, o * 512:(o + 1) * 512]),
                    waits=fw, inc=sDVE)
                t1done = sDVE.n
                pw = [(sDVE, t1done), xsl_ld]
                prevk = (o, tok - 2) if tok >= 2 else ((o - 1, NT + tok - 2) if o > 0 else None)
                if prevk is not None and prevk in yst_done:
                    pw.append(yst_done[prevk])
                p.emit("gpsimd", lambda e, o=o, tok=tok, tb=tb: e.tensor_add(
                    yout[tb], t1b[tb], attn_proj[:, tok, o * 512:(o + 1) * 512]),
                    waits=pw)
                p.emit("gpsimd", lambda e: e.drain())
                p.emit("gpsimd", lambda e, o=o, tok=tok, tb=tb, ob=ob: e.tensor_add(
                    yout[tb], yout[tb], xsl[ob][:, tok, :]), inc=sPOOL)
                fin_done[(o, tok)] = (sDVE, sPOOL.n)
                p.emit("gpsimd", lambda e, o=o, tok=tok, tb=tb: e.dma_start(
                    out=y_out.rearrange("(t p) d -> p t d", p=128)[:, tok, o * 512:(o + 1) * 512],
                    in_=yout[tb]),
                    waits=[(sPOOL, sPOOL.n)], inc=sST[tb], inc_n=16)
                yst_done[(o, tok)] = (sST[tb], sST[tb].n)
            dvet1_done[o] = sDVE.n
    p.emit("gpsimd", lambda e: e.nop(),
           waits=[(sST[0], sST[0].n), (sST[1], sST[1].n), (sSTB, sSTB.n)])

    with nc.Block() as block:
        @block.sync
        def _(eng):
            for f in p.ops["sync"]:
                f(eng)

        @block.tensor
        def _(eng):
            for f in p.ops["tensor"]:
                f(eng)

        @block.scalar
        def _(eng):
            for f in p.ops["scalar"]:
                f(eng)

        @block.vector
        def _(eng):
            for f in p.ops["vector"]:
                f(eng)

        @block.gpsimd
        def _(eng):
            for f in p.ops["gpsimd"]:
                f(eng)

    return nc


def _prev_h(qt, g, hl, back):
    idx = (qt * KVH + g) * 4 + hl - back
    if idx < 0:
        return None
    q2, rem = divmod(idx, KVH * 4)
    g2, hl2 = divmod(rem, 4)
    return (q2, g2, hl2)


def _host_prep(L0_attn_hidden_states, L0_attn_Wq, L0_attn_Wk, L0_attn_Wv,
               L0_attn_Wo, L0_ffn_gate_w, L0_ffn_gate_s, L0_ffn_up_w,
               L0_ffn_up_s, L0_ffn_down_w, L0_ffn_down_s):
    x = np.ascontiguousarray(np.asarray(L0_attn_hidden_states, dtype=np.float32)[0])
    Wq = np.asarray(L0_attn_Wq, dtype=np.float32)
    Wk = np.asarray(L0_attn_Wk, dtype=np.float32)
    Wv = np.asarray(L0_attn_Wv, dtype=np.float32)
    Wo = np.asarray(L0_attn_Wo, dtype=np.float32)
    gw = np.asarray(L0_ffn_gate_w, dtype=np.int32)
    uw = np.asarray(L0_ffn_up_w, dtype=np.int32)
    dw = np.asarray(L0_ffn_down_w, dtype=np.int32)
    gs = np.asarray(L0_ffn_gate_s, dtype=np.float32)
    us = np.asarray(L0_ffn_up_s, dtype=np.float32)
    ds = np.asarray(L0_ffn_down_s, dtype=np.float32)

    wq_b = np.ascontiguousarray(Wq.T.reshape(E, HQ, 128).transpose(1, 0, 2))
    wk_b = np.ascontiguousarray(Wk.T.reshape(E, KVH, 128).transpose(1, 0, 2))
    wv_t = np.ascontiguousarray(Wv.T)
    wo_b = np.ascontiguousarray(Wo.T.reshape(E, 8, 512).transpose(1, 0, 2))
    gw_b = np.ascontiguousarray(gw.T.reshape(E, FC, 128).transpose(1, 0, 2))
    uw_b = np.ascontiguousarray(uw.T.reshape(E, FC, 128).transpose(1, 0, 2))
    dw_b = np.ascontiguousarray(dw.T.reshape(FF, 8, 512).transpose(1, 0, 2))
    gs_t = np.ascontiguousarray(gs.reshape(FC, 128).T)
    us_t = np.ascontiguousarray(us.reshape(FC, 128).T)
    dsb = np.ascontiguousarray(np.broadcast_to(ds[None, :], (128, E)))

    inv_freq = (1.0 / (10000.0 ** (np.arange(0, HD, 2, dtype=np.float32) / HD)))
    t = np.arange(S, dtype=np.float32)
    freqs = t[:, None] * inv_freq[None, :]
    cos_fm = np.ascontiguousarray(np.cos(freqs).T.astype(np.float32))
    sin_fm = np.ascontiguousarray(np.sin(freqs).T.astype(np.float32))

    kpos = np.arange(S).reshape(NKT, 128)
    in_maps = []
    for c in range(NCORE):
        m = np.empty((2, NKT, 128, 512), dtype=ml_dtypes.bfloat16)
        for qt in range(2):
            qp = TPC * c + 512 * qt + np.arange(512)
            valid = (np.abs(kpos[:, :, None] - qp[None, None, :]) >= WIN)
            m[qt] = valid.astype(np.float32)
        in_maps.append({
            "x": np.ascontiguousarray(x[TPC * c:TPC * (c + 1)]),
            "wq": wq_b, "wk": wk_b, "wv": wv_t, "wo": wo_b,
            "gw": gw_b, "uw": uw_b, "dw": dw_b,
            "gs": gs_t, "us": us_t, "dsb": dsb,
            "cos64": np.ascontiguousarray(cos_fm[:, TPC * c:TPC * (c + 1)]),
            "sin64": np.ascontiguousarray(sin_fm[:, TPC * c:TPC * (c + 1)]),
            "masks": m,
        })
    return in_maps


def kernel(**inputs):
    in_maps = _host_prep(**inputs)
    nc = build_nc()
    res = run_bass_kernel_spmd(nc, in_maps, core_ids=list(range(NCORE)))
    y = np.concatenate([res.results[c]["y"] for c in range(NCORE)], axis=0)
    return y.reshape(1, S, E)


# version 08
# speedup vs baseline: 1.6801x; 1.1209x over previous
import math
from contextlib import ExitStack

import numpy as np
import ml_dtypes

import concourse.bass as bass
import concourse.mybir as mybir
from concourse.bass_utils import run_bass_kernel_spmd

F32 = mybir.dt.float32
BF16 = mybir.dt.bfloat16
I32 = mybir.dt.int32
AF = mybir.ActivationFunctionType
ALU = mybir.AluOpType

S, E = 8192, 4096
HQ, KVH, HD = 32, 8, 128
WIN = 1024
FF = 11008
NCORE = 8
TPC = S // NCORE
NT = TPC // 128
DC = E // 128
FC = FF // 128
NKT = S // 128
CSHIFT = 60.0
ISQ = 1.0 / math.sqrt(HD)
EPS = 1e-5


class Sem:
    def __init__(self, handle):
        self.h = handle
        self.n = 0


class Prog:

    ENGS = ("tensor", "scalar", "vector", "gpsimd", "sync")

    def __init__(self):
        self.ops = {e: [] for e in self.ENGS}
        self.waited = {e: {} for e in self.ENGS}

    def emit(self, eng, fn, waits=(), inc=None, inc_n=1):
        real_waits = []
        for (s, v) in waits:
            if v <= 0:
                continue
            prev = self.waited[eng].get(id(s), -1)
            if v > prev:
                real_waits.append((s, v))
                self.waited[eng][id(s)] = v
        if inc is not None:
            inc.n += inc_n

        def run(engine, fn=fn, real_waits=tuple(real_waits), inc=inc, inc_n=inc_n):
            for (s, v) in real_waits:
                engine.wait_ge(s.h, v)
            instr = fn(engine)
            if inc is not None:
                instr.then_inc(inc.h, inc_n)

        self.ops[eng].append(run)


def build_nc():
    nc = bass.Bass("TRN2", target_bir_lowering=False, dynamic_dma_scratch_size=24576)

    x_in = nc.dram_tensor("x", [TPC, E], F32, kind="ExternalInput").ap()
    wq_in = nc.dram_tensor("wq", [HQ, E, 128], F32, kind="ExternalInput").ap()
    wk_in = nc.dram_tensor("wk", [KVH, E, 128], F32, kind="ExternalInput").ap()
    wv_in = nc.dram_tensor("wv", [E, KVH * 128], F32, kind="ExternalInput").ap()
    wo_in = nc.dram_tensor("wo", [8, E, 512], F32, kind="ExternalInput").ap()
    gw_in = nc.dram_tensor("gw", [FC, E, 128], I32, kind="ExternalInput").ap()
    uw_in = nc.dram_tensor("uw", [FC, E, 128], I32, kind="ExternalInput").ap()
    dw_in = nc.dram_tensor("dw", [8, FF, 512], I32, kind="ExternalInput").ap()
    gs_in = nc.dram_tensor("gs", [128, FC], F32, kind="ExternalInput").ap()
    us_in = nc.dram_tensor("us", [128, FC], F32, kind="ExternalInput").ap()
    dsb_in = nc.dram_tensor("dsb", [128, E], F32, kind="ExternalInput").ap()
    cos_in = nc.dram_tensor("cos64", [64, TPC], F32, kind="ExternalInput").ap()
    sin_in = nc.dram_tensor("sin64", [64, TPC], F32, kind="ExternalInput").ap()
    msk_in = nc.dram_tensor("masks", [2, 12, 128, 512], BF16, kind="ExternalInput").ap()
    y_out = nc.dram_tensor("y", [TPC, E], F32, kind="ExternalOutput").ap()

    kself = nc.dram_tensor("kself", [KVH * 128, TPC], BF16).ap()
    kgdup = nc.dram_tensor("kgdup", [2 * NCORE * KVH * 128, TPC], BF16).ap()
    krot = nc.dram_tensor("krot", [NCORE * KVH * 128, TPC], BF16).ap()
    vself = nc.dram_tensor("vself", [NT, KVH, 128, 128], BF16).ap()
    vgdup = nc.dram_tensor("vgdup", [2 * NCORE, NT, KVH, 128, 128], BF16).ap()
    vrot = nc.dram_tensor("vrot", [NCORE, NT, KVH, 128, 128], BF16).ap()
    qbuf = nc.dram_tensor("qbuf", [HQ, 128, TPC], BF16).ap()
    abuf = nc.dram_tensor("abuf", [HQ, 128, TPC], BF16).ap()
    hbuf = nc.dram_tensor("hbuf", [FC, 128, TPC], BF16).ap()

    cos_s = nc.alloc_sbuf_tensor("cos_s", [64, TPC], F32).ap()
    sin_s = nc.alloc_sbuf_tensor("sin_s", [64, TPC], F32).ap()
    gs_s = nc.alloc_sbuf_tensor("gs_s", [128, FC], F32).ap()
    us_s = nc.alloc_sbuf_tensor("us_s", [128, FC], F32).ap()
    dsb_s = nc.alloc_sbuf_tensor("dsb_s", [128, E], F32).ap()
    ident = nc.alloc_sbuf_tensor("ident", [128, 128], BF16).ap()
    ones_col = nc.alloc_sbuf_tensor("ones_col", [128, 1], BF16).ap()
    ones_row = nc.alloc_sbuf_tensor("ones_row", [1, 128], F32).ap()
    c_scale = nc.alloc_sbuf_tensor("c_scale", [128, 1], F32).ap()
    c_bias = nc.alloc_sbuf_tensor("c_bias", [128, 1], F32).ap()
    c_eps = nc.alloc_sbuf_tensor("c_eps", [128, 1], F32).ap()
    ln_mv = nc.alloc_sbuf_tensor("ln_mv", [128, NT, 2], F32).ap()
    ln_std8 = nc.alloc_sbuf_tensor("ln_std8", [128, NT], F32).ap()
    ln_rstd8 = nc.alloc_sbuf_tensor("ln_rstd8", [128, NT], F32).ap()
    ln_nbias8 = nc.alloc_sbuf_tensor("ln_nbias8", [128, NT], F32).ap()
    ln_bnt = [nc.alloc_sbuf_tensor(f"ln_bnt{i}", [128, 48], F32).ap() for i in range(2)]
    ln_mean8 = nc.alloc_sbuf_tensor("ln_mean8", [128, NT], F32).ap()
    ln_var8 = nc.alloc_sbuf_tensor("ln_var8", [128, NT], F32).ap()

    BASE = (nc.sbuf_base + 31) & ~31
    TOP = nc.sbuf_top
    RESERVE = 2 * 1024
    avail = TOP - BASE - RESERVE
    assert avail >= 168 * 1024, f"overlay too small: {avail}"
    pad = BASE - nc.sbuf_base
    nc.alloc_sbuf_tensor("overlay_reserve", [128, pad + avail], mybir.dt.uint8)

    made = {}

    def at(name, off_bytes, shape, dtype):
        key = (name, off_bytes)
        if key not in made:
            nbytes = int(np.prod(shape[1:])) * mybir.dt.size(dtype)
            assert off_bytes % 32 == 0
            assert off_bytes + nbytes <= avail, (name, off_bytes, nbytes, avail)
            made[key] = nc.alloc_sbuf_tensor_at(
                f"{name}_{off_bytes}", shape, dtype, offset=BASE + off_bytes).ap()
        return made[key]

    KB = 1024

    p = Prog()

    def mksem(name):
        return Sem(nc.semaphore(name).__enter__())

    sCC = mksem("sCC")
    sPE = mksem("sPE")
    sACT = mksem("sACT")
    sDVE = mksem("sDVE")
    sPOOL = mksem("sPOOL")
    sConst = mksem("sConst")
    sXT = [mksem("sXT0"), mksem("sXT1")]
    sWS = [mksem("sWS0"), mksem("sWS1")]
    sGS = mksem("sGS")
    sUS = mksem("sUS")
    sKSL = [mksem("sKSL0"), mksem("sKSL1")]
    sVSL = [mksem("sVSL0"), mksem("sVSL1")]
    sQSB = [mksem("sQSB0"), mksem("sQSB1")]
    sMSK = mksem("sMSK")
    sST = [mksem("sST0"), mksem("sST1")]
    sA6 = [mksem(f"sA6_{i}") for i in range(4)]
    sD7 = [mksem(f"sD7_{i}") for i in range(4)]
    sH7 = [mksem(f"sH7_{i}") for i in range(4)]
    sSTB = mksem("sSTB")
    ALL_SEMS = tuple([sCC, sPE, sACT, sDVE, sPOOL, sConst, sXT[0], sXT[1],
                sWS[0], sWS[1], sGS, sUS, sKSL[0], sKSL[1], sVSL[0], sVSL[1],
                sQSB[0], sQSB[1], sMSK, sST[0], sST[1], sSTB]
                + sA6 + sD7 + sH7)

    def snap():
        return [(s, s.n) for s in ALL_SEMS]

    p.emit("sync", lambda e: e.dma_start(out=cos_s, in_=cos_in), inc=sConst, inc_n=16)
    p.emit("sync", lambda e: e.dma_start(out=sin_s, in_=sin_in), inc=sConst, inc_n=16)
    p.emit("sync", lambda e: e.dma_start(out=gs_s, in_=gs_in), inc=sConst, inc_n=16)
    p.emit("sync", lambda e: e.dma_start(out=us_s, in_=us_in), inc=sConst, inc_n=16)
    p.emit("sync", lambda e: e.dma_start(out=dsb_s, in_=dsb_in), inc=sConst, inc_n=16)
    p.emit("vector", lambda e: e.memset(ident, 0.0))
    p.emit("vector", lambda e: e.memset(ones_col, 1.0))
    p.emit("vector", lambda e: e.memset(ones_row, 1.0))
    p.emit("vector", lambda e: e.memset(c_scale, ISQ))
    p.emit("vector", lambda e: e.memset(c_bias, -CSHIFT))
    p.emit("vector", lambda e: e.memset(c_eps, EPS), inc=sDVE)
    dve_const_done = sDVE.n
    p.emit("gpsimd", lambda e: e.affine_select(
        out=ident, in_=ident, compare_op=ALU.not_equal, fill=1.0,
        base=0, pattern=[[-1, 128]], channel_multiplier=1),
        waits=[(sDVE, dve_const_done)], inc=sPOOL)
    ident_done = sPOOL.n

    xtile = [at("xtile", i * 16 * KB, [128, E], F32) for i in range(2)]
    xsq = at("xsq", 32 * KB, [128, E], F32)
    xn_tm = [at("xn_tm", 48 * KB + i * 8 * KB, [128, E], BF16) for i in range(2)]
    xn_fm = at("xn_fm", 64 * KB, [128, DC, TPC], BF16)
    XN_OFF, XN_END = 64 * KB, 128 * KB

    with nc.psum_tensor("trp0", [128, 128], BF16) as trp0_t, \
         nc.psum_tensor("trp1", [128, 128], BF16) as trp1_t:
        trp = [trp0_t.ap(), trp1_t.ap()]
        xload_done = [0] * NT
        stats_done = [0] * NT
        for it in range(NT):
            b = it % 2
            waits = [(sDVE, stats_done[it - 2])] if it >= 2 else []
            p.emit("sync", lambda e, b=b, it=it: e.dma_start(
                out=xtile[b], in_=x_in[it * 128:(it + 1) * 128, :]),
                waits=waits, inc=sXT[b], inc_n=16)
            xload_done[it] = (sXT[b], sXT[b].n)
            for ch in range(8):
                p.emit("vector", lambda e, b=b, ch=ch: e.bn_stats(
                    ln_bnt[b][:, ch * 6:(ch + 1) * 6],
                    xtile[b][:, ch * 512:(ch + 1) * 512]),
                    waits=[xload_done[it]] if ch == 0 else ())
            p.emit("vector", lambda e: e.drain())
            p.emit("vector", lambda e, b=b, it=it: e.bn_aggr(
                ln_mv[:, it, :], ln_bnt[b]), inc=sDVE)
            stats_done[it] = sDVE.n
        p.emit("vector", lambda e: e.tensor_copy(out=ln_mean8, in_=ln_mv[:, :, 0]),
               waits=[(sDVE, stats_done[NT - 1])])
        p.emit("vector", lambda e: e.tensor_copy(out=ln_var8, in_=ln_mv[:, :, 1]), inc=sDVE)
        mv_done = sDVE.n
        p.emit("scalar", lambda e: e.activation(
            out=ln_std8, in_=ln_var8, func=AF.Sqrt,
            bias=c_eps[:, 0:1], scale=1.0),
            waits=[(sDVE, mv_done)], inc=sACT)
        sqrt_done = sACT.n
        p.emit("vector", lambda e: e.reciprocal(ln_rstd8, ln_std8),
               waits=[(sACT, sqrt_done)])
        p.emit("vector", lambda e: e.drain())
        p.emit("vector", lambda e: e.scalar_tensor_tensor(
            out=ln_nbias8, in0=ln_mean8, scalar=-1.0,
            in1=ln_rstd8, op0=ALU.mult, op1=ALU.mult), inc=sDVE)
        p.emit("vector", lambda e: e.drain())
        lnb_done = sDVE.n
        xn_done = [0] * NT
        tr_done = [0] * NT
        ev_done = [0] * (NT * DC)
        for it in range(NT):
            b = it % 2
            w = [(sDVE, stats_done[it])]
            if it >= 2:
                w.append((sDVE, xn_done[it - 2]))
            p.emit("sync", lambda e, b=b, it=it: e.dma_start(
                out=xtile[b], in_=x_in[it * 128:(it + 1) * 128, :]),
                waits=w, inc=sXT[b], inc_n=16)
            xl = (sXT[b], sXT[b].n)
            xn_waits = [xl, (sDVE, lnb_done)]
            if it >= 2:
                xn_waits.append((sPE, tr_done[it - 2]))
            p.emit("vector", lambda e, b=b, it=it: e.tensor_scalar(
                out=xn_tm[b], in0=xtile[b],
                scalar1=ln_rstd8[:, it:it + 1], scalar2=ln_nbias8[:, it:it + 1],
                op0=ALU.mult, op1=ALU.add),
                waits=xn_waits, inc=sDVE)
            xn_done[it] = sDVE.n
            for d in range(DC):
                tw = [(sDVE, xn_done[it]), (sPOOL, ident_done)]
                prev = it * DC + d - 2
                if prev >= 0:
                    tw.append((sDVE, ev_done[prev]))
                p.emit("tensor", lambda e, b=b, d=d: e.transpose(
                    trp[d % 2], xn_tm[b][:, d * 128:(d + 1) * 128], ident),
                    waits=tw, inc=sPE)
                tr_done[it] = sPE.n
                p.emit("vector", lambda e, it=it, d=d: e.tensor_copy(
                    out=xn_fm[:, d, it * 128:(it + 1) * 128], in_=trp[d % 2]),
                    waits=[(sPE, sPE.n)], inc=sDVE)
                ev_done[it * DC + d] = sDVE.n
        xn_ready = sDVE.n
    V_tm = at("V_tm", 0, [128, NT, TPC], BF16)
    K_fm = at("K_fm", 16 * KB, [128, KVH, TPC], BF16)
    wstage = [at("wstage", 32 * KB + i * 16 * KB, [128, 8, 512], F32) for i in range(2)]
    wv_bf = at("wv_bf", 128 * KB, [128, DC, 512], BF16)
    wkstage = [at("wkstage", 32 * KB + i * 16 * KB, [128, DC, 128], F32) for i in range(2)]
    wk_bf = [at("wk_bf", 128 * KB + i * 8 * KB, [128, DC, 128], BF16) for i in range(2)]
    rsc = [at("rsc", 148 * KB + i * 2 * KB, [64, 512], F32) for i in range(4)]

    s2bar = snap()
    with nc.psum_tensor("pv", [128, 2, 512], F32) as pv_t, \
         nc.psum_tensor("pk", [128, 2, 2, 512], F32) as pk_t:
        pv = pv_t.ap()
        pk = pk_t.ap()
        vconv_done = [0] * 8
        vmm_done = [0] * (2 * NT)
        vevac_done = [0] * (2 * NT)
        for vc in range(2):
            for dg in range(4):
                b = dg % 2
                w = list(s2bar) if (vc == 0 and dg < 2) else []
                if vc * 4 + dg >= 2:
                    w.append((sACT, vconv_done[vc * 4 + dg - 2]))
                p.emit("sync", lambda e, b=b, vc=vc, dg=dg: e.dma_start(
                    out=wstage[b],
                    in_=wv_in.rearrange("(c p) n -> p c n", p=128)[:, dg * 8:(dg + 1) * 8, vc * 512:(vc + 1) * 512]),
                    waits=w, inc=sWS[b], inc_n=16)
                p.emit("scalar", lambda e, b=b, dg=dg: e.activation(
                    out=wv_bf[:, dg * 8:(dg + 1) * 8, :], in_=wstage[b], func=AF.Copy),
                    waits=[(sWS[b], sWS[b].n)] + (list(s2bar) if (vc == 0 and dg == 0) else []),
                    inc=sACT)
                vconv_done[vc * 4 + dg] = sACT.n
            for it in range(NT):
                b = it % 2
                idx = vc * NT + it
                mw = [(sDVE, xn_ready), (sACT, vconv_done[vc * 4 + 3])]
                if idx >= 2:
                    mw.append((sACT, vevac_done[idx - 2]))
                for d in range(DC):
                    p.emit("tensor", lambda e, b=b, it=it, d=d: e.matmul(
                        pv[:, b, :], xn_fm[:, d, it * 128:(it + 1) * 128],
                        wv_bf[:, d, :], start=(d == 0), stop=(d == DC - 1)),
                        waits=mw if d == 0 else (), inc=(sPE if d == DC - 1 else None))
                vmm_done[idx] = sPE.n
                p.emit("scalar", lambda e, b=b, it=it, vc=vc: e.activation(
                    out=V_tm[:, it, vc * 512:(vc + 1) * 512], in_=pv[:, b, :], func=AF.Copy),
                    waits=[(sPE, vmm_done[idx])], inc=sACT)
                vevac_done[idx] = sACT.n
        for kt2 in range(NT):
            p.emit("gpsimd", lambda e, kt2=kt2: e.dma_start(
                out=vself[kt2].rearrange("g p v -> p g v"),
                in_=V_tm[:, kt2, :].rearrange("p (g v) -> p g v", g=KVH)),
                waits=[(sACT, vevac_done[NT + kt2])], inc=sSTB, inc_n=16)
        vstore_done = sSTB.n
        kconv_done = [0] * KVH
        kmm_done = [0] * KVH
        krope_done = [0] * KVH
        for kc in range(KVH):
            b = kc % 2
            w = [(sACT, vconv_done[7])] if kc < 2 else [(sACT, kconv_done[kc - 2])]
            p.emit("sync", lambda e, b=b, kc=kc: e.dma_start(
                out=wkstage[b], in_=wk_in[kc].rearrange("(c p) m -> p c m", p=128)),
                waits=w, inc=sWS[b], inc_n=16)
            cw = [(sWS[b], sWS[b].n)]
            if kc >= 2:
                cw.append((sPE, kmm_done[kc - 2]))
            p.emit("scalar", lambda e, b=b: e.activation(
                out=wk_bf[b], in_=wkstage[b], func=AF.Copy), waits=cw, inc=sACT)
            kconv_done[kc] = sACT.n
            mw = [(sACT, kconv_done[kc]), (sDVE, xn_ready)]
            if kc >= 2:
                mw.append((sDVE, krope_done[kc - 2]))
            for tt in range(2):
                for d in range(DC):
                    p.emit("tensor", lambda e, b=b, tt=tt, d=d: e.matmul(
                        pk[:, b, tt, :], wk_bf[b][:, d, :],
                        xn_fm[:, d, tt * 512:(tt + 1) * 512],
                        start=(d == 0), stop=(d == DC - 1)),
                        waits=mw if (tt == 0 and d == 0) else (),
                        inc=(sPE if (tt == 1 and d == DC - 1) else None))
            kmm_done[kc] = sPE.n
            for tt in range(2):
                cosl = cos_s[:, tt * 512:(tt + 1) * 512]
                sinl = sin_s[:, tt * 512:(tt + 1) * 512]
                sl = slice(tt * 512, (tt + 1) * 512)
                p.emit("vector", lambda e, b=b, tt=tt, cosl=cosl: e.tensor_mul(
                    rsc[0], pk[0:64, b, tt, :], cosl),
                    waits=[(sPE, kmm_done[kc]), (sConst, 5 * 16)])
                p.emit("vector", lambda e, b=b, tt=tt, sinl=sinl: e.tensor_mul(
                    rsc[1], pk[64:128, b, tt, :], sinl))
                p.emit("vector", lambda e: e.drain())
                p.emit("vector", lambda e, kc=kc, sl=sl: e.tensor_sub(
                    K_fm[0:64, kc, sl], rsc[0], rsc[1]))
                p.emit("vector", lambda e, b=b, tt=tt, cosl=cosl: e.tensor_mul(
                    rsc[2], pk[64:128, b, tt, :], cosl))
                p.emit("vector", lambda e, b=b, tt=tt, sinl=sinl: e.tensor_mul(
                    rsc[3], pk[0:64, b, tt, :], sinl))
                p.emit("vector", lambda e: e.drain())
                p.emit("vector", lambda e, kc=kc, sl=sl: e.tensor_add(
                    K_fm[64:128, kc, sl], rsc[2], rsc[3]),
                    inc=(sDVE if tt == 1 else None))
            krope_done[kc] = sDVE.n
        p.emit("gpsimd", lambda e: e.dma_start(
            out=kself.rearrange("(g p) t -> p g t", p=128), in_=K_fm),
            waits=[(sDVE, krope_done[KVH - 1])], inc=sSTB, inc_n=16)
        kvstore_total = sSTB.n
        p.emit("gpsimd", lambda e: e.collective_compute(
            "AllGather", ALU.bypass,
            replica_groups=[list(range(NCORE))],
            ins=[kself.opt()], outs=[kgdup[0:NCORE * KVH * 128, :].opt()]),
            waits=[(sSTB, kvstore_total)], inc=sCC)
        p.emit("gpsimd", lambda e: e.collective_compute(
            "AllGather", ALU.bypass,
            replica_groups=[list(range(NCORE))],
            ins=[vself.rearrange("a g p v -> (a g p) v").opt()],
            outs=[vgdup.rearrange("r a g p v -> (r a g p) v")[0:NCORE * NT * KVH * 128, :].opt()]),
            waits=[(sSTB, kvstore_total)], inc=sCC)
        cc_done = sCC.n
        HALFK = NCORE * KVH * 128
        p.emit("gpsimd", lambda e: e.dma_start(
            out=kgdup[HALFK:2 * HALFK, :], in_=kgdup[0:HALFK, :]),
            waits=[(sCC, cc_done)], inc=sSTB, inc_n=16)
        vgf = vgdup.rearrange("r a g p v -> (r a g p) v")
        HALFV = NCORE * NT * KVH * 128
        p.emit("gpsimd", lambda e: e.dma_start(
            out=vgf[HALFV:2 * HALFV, :], in_=vgf[0:HALFV, :]),
            inc=sSTB, inc_n=16)
        dup_done = sSTB.n
        kgdupv = kgdup.rearrange("(r x) t -> r (x t)", x=KVH * 128)
        krotv = krot.rearrange("(r x) t -> r (x t)", x=KVH * 128)
        vgdupv = vgdup.rearrange("r a g p v -> r (a g p v)")
        vrotv = vrot.rearrange("r a g p v -> r (a g p v)")
        _rotpid = {}

        def _pid(e):
            if "p" not in _rotpid:
                _rotpid["p"] = nc.partition_id([mybir.EngineType.Pool])
            return _rotpid["p"]

        for rp in range(NCORE):
            p.emit("gpsimd", lambda e, rp=rp: e.dma_start(
                out=krotv[rp:rp + 1, :],
                in_=kgdupv[bass.ts(_pid(e) + rp, 1)]),
                waits=([(sSTB, dup_done)] if rp == 0 else ()), inc=sSTB, inc_n=16)
        for rp in range(NCORE):
            p.emit("gpsimd", lambda e, rp=rp: e.dma_start(
                out=vrotv[rp:rp + 1, :],
                in_=vgdupv[bass.ts(_pid(e) + rp, 1)]),
                inc=sSTB, inc_n=16)
        rot_done = (sSTB, sSTB.n)

    qchunk = [at("qchunk", 144 * KB + i * 2 * KB, [128, TPC], BF16) for i in range(2)]
    s3bar = snap()
    with nc.psum_tensor("pq", [128, 2, 2, 512], F32) as pq_t:
        pq = pq_t.ap()
        qconv_done = [0] * HQ
        qmm_done = [0] * HQ
        qst_done = [0] * HQ
        qrope_done = [0] * HQ
        for qc in range(HQ):
            b = qc % 2
            w = [(sACT, qconv_done[qc - 2])] if qc >= 2 else list(s3bar)
            p.emit("sync", lambda e, b=b, qc=qc: e.dma_start(
                out=wkstage[b], in_=wq_in[qc].rearrange("(c p) m -> p c m", p=128)),
                waits=w, inc=sWS[b], inc_n=16)
            cw = [(sWS[b], sWS[b].n)]
            if qc >= 2:
                cw.append((sPE, qmm_done[qc - 2]))
            else:
                cw += list(s3bar)
            p.emit("scalar", lambda e, b=b: e.activation(
                out=wk_bf[b], in_=wkstage[b], func=AF.Copy), waits=cw, inc=sACT)
            qconv_done[qc] = sACT.n
            mw = [(sACT, qconv_done[qc])]
            if qc >= 2:
                mw.append((sDVE, qrope_done[qc - 2]))
            for tt in range(2):
                for d in range(DC):
                    p.emit("tensor", lambda e, b=b, tt=tt, d=d: e.matmul(
                        pq[:, b, tt, :], wk_bf[b][:, d, :],
                        xn_fm[:, d, tt * 512:(tt + 1) * 512],
                        start=(d == 0), stop=(d == DC - 1)),
                        waits=mw if (tt == 0 and d == 0) else (),
                        inc=(sPE if (tt == 1 and d == DC - 1) else None))
            qmm_done[qc] = sPE.n
            rw = [(sPE, qmm_done[qc])]
            if qc >= 2:
                rw.append(qst_done[qc - 2])
            for tt in range(2):
                cosl = cos_s[:, tt * 512:(tt + 1) * 512]
                sinl = sin_s[:, tt * 512:(tt + 1) * 512]
                sl = slice(tt * 512, (tt + 1) * 512)
                p.emit("vector", lambda e, b=b, tt=tt, cosl=cosl: e.tensor_mul(
                    rsc[0], pq[0:64, b, tt, :], cosl), waits=rw if tt == 0 else ())
                p.emit("vector", lambda e, b=b, tt=tt, sinl=sinl: e.tensor_mul(
                    rsc[1], pq[64:128, b, tt, :], sinl))
                p.emit("vector", lambda e: e.drain())
                p.emit("vector", lambda e, b=b, sl=sl: e.tensor_sub(
                    qchunk[b][0:64, sl], rsc[0], rsc[1]))
                p.emit("vector", lambda e, b=b, tt=tt, cosl=cosl: e.tensor_mul(
                    rsc[2], pq[64:128, b, tt, :], cosl))
                p.emit("vector", lambda e, b=b, tt=tt, sinl=sinl: e.tensor_mul(
                    rsc[3], pq[0:64, b, tt, :], sinl))
                p.emit("vector", lambda e: e.drain())
                p.emit("vector", lambda e, b=b, sl=sl: e.tensor_add(
                    qchunk[b][64:128, sl], rsc[2], rsc[3]),
                    inc=(sDVE if tt == 1 else None))
            qrope_done[qc] = sDVE.n
            p.emit("gpsimd", lambda e, b=b, qc=qc: e.dma_start(
                out=qbuf[qc], in_=qchunk[b]),
                waits=[(sDVE, qrope_done[qc])], inc=sST[b], inc_n=16)
            qst_done[qc] = (sST[b], sST[b].n)
        qspill_done = [(sST[0], sST[0].n), (sST[1], sST[1].n)]

    gstage = at("gstage", 0, [128, DC, 128], I32)
    ustage = at("ustage", 16 * KB, [128, DC, 128], I32)
    g_bf = [at("g_bf", 32 * KB + i * 8 * KB, [128, DC, 128], BF16) for i in range(2)]
    u_bf = [at("u_bf", 48 * KB + i * 8 * KB, [128, DC, 128], BF16) for i in range(2)]
    tg_s = at("tg_s", 132 * KB, [128, TPC], BF16)
    h_t = [at("h_t", 128 * KB + i * 2 * KB, [128, TPC], BF16) for i in range(2)]
    s4bar = snap()
    with nc.psum_tensor("pg", [128, 2, TPC], F32) as pg_t, \
         nc.psum_tensor("pu", [128, 2, TPC], F32) as pu_t:
        pg = pg_t.ap()
        pu = pu_t.ap()
        gconv_done = [0] * FC
        uconv_done = [0] * FC
        fmm_done = [0] * FC
        hmul_done = [0] * FC
        hst_done = [0] * FC
        for f in range(FC):
            b = f % 2
            lw = [(sDVE, gconv_done[f - 1])] if f >= 1 else list(s4bar)
            p.emit("sync", lambda e, f=f: e.dma_start(
                out=gstage, in_=gw_in[f].rearrange("(c p) m -> p c m", p=128)),
                waits=lw, inc=sGS, inc_n=16)
            gload = (sGS, sGS.n)
            uw_w = [(sACT, uconv_done[f - 1])] if f >= 1 else list(s4bar)
            p.emit("sync", lambda e, f=f: e.dma_start(
                out=ustage, in_=uw_in[f].rearrange("(c p) m -> p c m", p=128)),
                waits=uw_w, inc=sUS, inc_n=16)
            uload = (sUS, sUS.n)
            gw_w = [gload]
            uw_w2 = [uload]
            if f >= 2:
                gw_w.append((sPE, fmm_done[f - 2]))
                uw_w2.append((sPE, fmm_done[f - 2]))
            if f < 2:
                gw_w += list(s4bar)
                uw_w2 += list(s4bar)
            p.emit("vector", lambda e, b=b: e.tensor_copy(out=g_bf[b], in_=gstage),
                   waits=gw_w, inc=sDVE)
            gconv_done[f] = sDVE.n
            p.emit("scalar", lambda e, b=b: e.activation(
                out=u_bf[b], in_=ustage, func=AF.Copy), waits=uw_w2, inc=sACT)
            uconv_done[f] = sACT.n
            mw = [(sDVE, gconv_done[f]), (sACT, uconv_done[f])]
            if f >= 2:
                mw.append((sDVE, hmul_done[f - 2]))
            for d in range(DC):
                for tt in range(2):
                    p.emit("tensor", lambda e, b=b, d=d, tt=tt: e.matmul(
                        pg[:, b, tt * 512:(tt + 1) * 512], g_bf[b][:, d, :],
                        xn_fm[:, d, tt * 512:(tt + 1) * 512],
                        start=(d == 0), stop=(d == DC - 1), skip_group_check=True),
                        waits=mw if (d == 0 and tt == 0) else ())
            for d in range(DC):
                for tt in range(2):
                    p.emit("tensor", lambda e, b=b, d=d, tt=tt: e.matmul(
                        pu[:, b, tt * 512:(tt + 1) * 512], u_bf[b][:, d, :],
                        xn_fm[:, d, tt * 512:(tt + 1) * 512],
                        start=(d == 0), stop=(d == DC - 1), skip_group_check=True),
                        inc=(sPE if (d == DC - 1 and tt == 1) else None))
            fmm_done[f] = sPE.n
            ew = [(sPE, fmm_done[f])]
            if f >= 1:
                ew.append((sDVE, hmul_done[f - 1]))
            if f >= 2:
                ew.append(hst_done[f - 2])
            p.emit("scalar", lambda e, f=f, b=b: e.activation(
                out=tg_s, in_=pg[:, b], func=AF.Silu, scale=gs_s[:, f:f + 1]),
                waits=ew, inc=sACT)
            tg_done = sACT.n
            p.emit("vector", lambda e, b=b, f=f: e.scalar_tensor_tensor(
                out=h_t[b], in0=pu[:, b], scalar=us_s[:, f:f + 1], in1=tg_s,
                op0=ALU.mult, op1=ALU.mult),
                waits=[(sACT, tg_done), (sPE, fmm_done[f])], inc=sDVE)
            hmul_done[f] = sDVE.n
            p.emit("gpsimd", lambda e, b=b, f=f: e.dma_start(
                out=hbuf[f], in_=h_t[b]),
                waits=[(sDVE, hmul_done[f])], inc=sST[b], inc_n=16)
            hst_done[f] = (sST[b], sST[b].n)
        hall_done = [(sST[0], sST[0].n), (sST[1], sST[1].n)]

    masks_s = at("masks_s", 0, [128, 12, 512], BF16)
    kslab = [at("kslab", 64 * KB + i * 16 * KB, [128, NCORE, TPC], BF16) for i in range(2)]
    vslab = [at("vslab", 96 * KB + i * 16 * KB, [128, NKT, 128], BF16) for i in range(2)]
    qsb = [at("qsb", 128 * KB + i * KB, [128, 512], BF16) for i in range(2)]
    ptb = [at("ptb", 130 * KB + i * 2 * KB, [128, 2, 512], BF16) for i in range(3)]
    bc_s = [at("bc_s", 136 * KB + i * 2 * KB, [128, 512], F32) for i in range(2)]
    rec_s = [at("rec_s", 140 * KB + i * 2 * KB, [1, 512], F32) for i in range(2)]
    aout = [at("aout", 144 * KB + i * KB, [128, 512], BF16) for i in range(2)]

    s5bar = snap()
    plans = _attn_plans()
    LAG = 2
    kgv = krot.rearrange("(r gg p) t -> gg p r t", gg=KVH, p=128)
    vgv = vrot.rearrange("r a gg p v -> gg p (r a) v")
    with nc.psum_tensor("stp", [128, 3, 2, 512], F32) as stp_t, \
         nc.psum_tensor("avp", [128, 512], F32) as avp_t, \
         nc.psum_tensor("dnbc", [128, 512], F32) as dnbc_t:
        stp = stp_t.ap()
        avp = avp_t.ap()
        dnbc = dnbc_t.ap()
        dnp = dnbc[0:1, :]
        mask_ld = [0, 0]
        slab_ld = {}
        q_ld = {}
        score_done = {}
        hfin = {}
        prev_exp_par = None
        for qt in range(2):
            pairs, mask_pairs = plans[qt]
            npr = len(pairs)
            mw = list(s5bar) if qt == 0 else [(sPE, sPE.n)]
            p.emit("gpsimd", lambda e, qt=qt: e.dma_start(
                out=masks_s,
                in_=msk_in[qt].rearrange("k p q -> p k q")),
                waits=mw, inc=sMSK, inc_n=16)
            mask_ld[qt] = (sMSK, sMSK.n)
            for g in range(KVH):
                b = g % 2
                sw = [rot_done]
                if qt == 0 and g < 2:
                    sw += list(s5bar)
                key_prev = (qt, g - 2) if g >= 2 else ((qt - 1, g + KVH - 2) if qt > 0 else None)
                if key_prev in score_done:
                    sw.append((sPE, score_done[key_prev]))
                p.emit("gpsimd", lambda e, g=g, b=b: e.dma_start(
                    out=kslab[b], in_=kgv[g]),
                    waits=sw, inc=sKSL[b], inc_n=16)
                for half in range(2):
                    p.emit("gpsimd", lambda e, g=g, b=b, half=half: e.dma_start(
                        out=vslab[b][:, half * 32:(half + 1) * 32, :],
                        in_=vgv[g][:, half * 32:(half + 1) * 32, :]),
                        inc=sVSL[b], inc_n=16)
                slab_ld[(qt, g)] = ((sKSL[b], sKSL[b].n), (sVSL[b], sVSL[b].n))
                for hl in range(4):
                    h = 4 * g + hl
                    hb = h % 2
                    qw = list(qspill_done)
                    hkey = (qt, g, hl)
                    pkey = _prev_h(qt, g, hl, 2)
                    if pkey in score_done:
                        qw.append((sPE, score_done[pkey]))
                    p.emit("gpsimd", lambda e, h=h, hb=hb, qt=qt: e.dma_start(
                        out=qsb[hb], in_=qbuf[h][:, qt * 512:(qt + 1) * 512]),
                        waits=qw, inc=sQSB[hb], inc_n=16)
                    q_ld[hkey] = (sQSB[hb], sQSB[hb].n)
                    p1 = _prev_h(qt, g, hl, 1)
                    exp_done = {}
                    rdy_done = {}

                    def emit_sc(idx, k, b=b, hb=hb, hkey=hkey, qt=qt, g=g,
                                exp_done=exp_done, prev_par=prev_exp_par):
                        par = idx % 3
                        w = []
                        if idx == 0:
                            w += [slab_ld[(qt, g)][0], slab_ld[(qt, g)][1], q_ld[hkey]]
                        if idx >= 3:
                            w.append((sACT, exp_done[idx - 3]))
                        elif prev_par is not None:
                            w.append((sACT, prev_par[par]))
                        for j in range(2):
                            kt = 2 * k + j
                            r, kt2 = kt // 8, kt % 8
                            p.emit("tensor",
                                   lambda e, par=par, j=j, r=r, kt2=kt2, hb=hb, b=b: e.matmul(
                                       stp[:, par, j, :],
                                       kslab[b][:, r, kt2 * 128:(kt2 + 1) * 128],
                                       qsb[hb], start=True, stop=True,
                                       skip_group_check=True),
                                   waits=w if j == 0 else (),
                                   inc=(sPE if j == 1 else None))
                        return sPE.n

                    def emit_av(idx, k, b=b, rdy_done=rdy_done):
                        par = idx % 3
                        w = [rdy_done[idx]]
                        if idx == 0 and p1 in hfin:
                            w.append((sDVE, hfin[p1][0]))
                        for j in range(2):
                            kt = 2 * k + j
                            p.emit("tensor",
                                   lambda e, par=par, j=j, kt=kt, b=b: e.matmul(
                                       avp, vslab[b][:, kt, :], ptb[par][:, j, :],
                                       start=(idx == 0 and j == 0),
                                       stop=(idx == npr - 1 and j == 1),
                                       skip_group_check=True),
                                   waits=w if j == 0 else ())
                            p.emit("tensor",
                                   lambda e, par=par, j=j: e.matmul(
                                       dnp, ones_col, ptb[par][:, j, :],
                                       start=(idx == 0 and j == 0),
                                       stop=(idx == npr - 1 and j == 1),
                                       skip_group_check=True),
                                   inc=(sPE if j == 1 else None))

                    for idx, (k, mp) in enumerate(pairs):
                        par = idx % 3
                        sc_pair = emit_sc(idx, k)
                        p.emit("scalar", lambda e, par=par: e.activation(
                            out=ptb[par].rearrange("p a q -> p (a q)"),
                            in_=stp[:, par].rearrange("p a q -> p (a q)"),
                            func=AF.Exp, scale=c_scale[:, 0:1], bias=c_bias[:, 0:1]),
                            waits=[(sPE, sc_pair)], inc=sACT)
                        exp_done[idx] = sACT.n
                        if mp is not None:
                            p.emit("vector", lambda e, par=par, mp=mp: e.tensor_mul(
                                ptb[par].rearrange("p a q -> p (a q)"),
                                ptb[par].rearrange("p a q -> p (a q)"),
                                masks_s[:, 2 * mp:2 * mp + 2, :].rearrange("p a q -> p (a q)")),
                                waits=[(sACT, exp_done[idx]), mask_ld[qt]], inc=sDVE)
                            rdy_done[idx] = (sDVE, sDVE.n)
                        else:
                            rdy_done[idx] = (sACT, exp_done[idx])
                        if idx >= LAG:
                            emit_av(idx - LAG, pairs[idx - LAG][0])
                    for idx in range(npr - LAG, npr):
                        emit_av(idx, pairs[idx][0])
                    p.emit("vector", lambda e, hb=hb: e.reciprocal(rec_s[hb], dnp),
                           waits=[(sPE, sPE.n)], inc=sDVE)
                    rec_done = sDVE.n
                    p.emit("tensor", lambda e, hb=hb: e.matmul(
                        dnbc, ones_row, rec_s[hb], start=True, stop=True,
                        skip_group_check=True),
                        waits=[(sDVE, rec_done)], inc=sPE)
                    bc_mm = sPE.n
                    p.emit("scalar", lambda e, hb=hb: e.activation(
                        out=bc_s[hb], in_=dnbc, func=AF.Copy),
                        waits=[(sPE, bc_mm)], inc=sACT)
                    bc_ev = sACT.n
                    aow = [(sACT, bc_ev)]
                    if p1 in hfin:
                        aow.append(hfin[p1][1])
                    p.emit("vector", lambda e, hb=hb: e.tensor_mul(
                        aout[hb], avp, bc_s[hb]), waits=aow, inc=sDVE)
                    av_used = sDVE.n
                    p.emit("gpsimd", lambda e, h=h, hb=hb, qt=qt: e.dma_start(
                        out=abuf[h][:, qt * 512:(qt + 1) * 512], in_=aout[hb]),
                        waits=[(sDVE, av_used)], inc=sST[hb], inc_n=16)
                    hfin[hkey] = (av_used, (sST[hb], sST[hb].n))
                    score_done[hkey] = sPE.n
                    prev_exp_par = {kk % 3: exp_done[kk] for kk in range(npr - 3, npr)}
        attn_done = [(sST[0], sST[0].n), (sST[1], sST[1].n)]

    attn_proj = at("attn_proj", 0, [128, NT, E], BF16)
    aslab = [at("aslab", 64 * KB + i * 2 * KB, [128, TPC], BF16) for i in range(4)]
    wostage = [at("wostage", 72 * KB + i * 16 * KB, [128, 8, 512], F32) for i in range(2)]
    wo_bf = at("wo_bf", 104 * KB, [128, DC, 512], BF16)

    s6bar = snap()
    with nc.psum_tensor("wop", [128, NT, 512], F32) as wop_t:
        wop = wop_t.ap()
        woconv_done = [0] * (8 * 4)
        aslab_ld = [0] * (8 * DC)
        womm_done = [0] * 8
        woev_done = [0] * 8
        for e_blk in range(8):
            for dg in range(4):
                b = dg % 2
                idx = e_blk * 4 + dg
                w = [(sACT, woconv_done[idx - 2])] if idx >= 2 else list(s6bar)
                p.emit("sync", lambda e, e_blk=e_blk, dg=dg, b=b: e.dma_start(
                    out=wostage[b],
                    in_=wo_in[e_blk].rearrange("(c p) n -> p c n", p=128)[:, dg * 8:(dg + 1) * 8, :]),
                    waits=w, inc=sWS[b], inc_n=16)
                cw = [(sWS[b], sWS[b].n)]
                if e_blk > 0 and dg < 2:
                    cw.append((sPE, womm_done[e_blk - 1]))
                if e_blk == 0 and dg == 0:
                    cw += list(s6bar)
                p.emit("scalar", lambda e, dg=dg, b=b: e.activation(
                    out=wo_bf[:, dg * 8:(dg + 1) * 8, :], in_=wostage[b], func=AF.Copy),
                    waits=cw, inc=sACT)
                woconv_done[idx] = sACT.n
            for hd in range(DC):
                b = hd % 4
                aw = list(attn_done)
                if e_blk * DC + hd >= 4:
                    aw.append((sPE, aslab_ld[e_blk * DC + hd - 4]))
                if e_blk == 0 and hd < 4:
                    aw += list(s6bar)
                p.emit("gpsimd", lambda e, hd=hd, b=b: e.dma_start(
                    out=aslab[b], in_=abuf[hd]), waits=aw, inc=sA6[b], inc_n=16)
                asl = (sA6[b], sA6[b].n)
                mw = [asl, (sACT, woconv_done[e_blk * 4 + hd // 8])]
                if hd == 0 and e_blk > 0:
                    mw.append((sACT, woev_done[e_blk - 1]))
                for tok in range(NT):
                    p.emit("tensor", lambda e, b=b, hd=hd, tok=tok: e.matmul(
                        wop[:, tok, :], aslab[b][:, tok * 128:(tok + 1) * 128],
                        wo_bf[:, hd, :], start=(hd == 0), stop=(hd == DC - 1),
                        skip_group_check=True),
                        waits=mw if tok == 0 else (),
                        inc=(sPE if tok == NT - 1 else None))
                aslab_ld[e_blk * DC + hd] = sPE.n
            womm_done[e_blk] = sPE.n
            for tok in range(NT):
                p.emit("scalar", lambda e, tok=tok, e_blk=e_blk: e.activation(
                    out=attn_proj[:, tok, e_blk * 512:(e_blk + 1) * 512],
                    in_=wop[:, tok, :], func=AF.Copy),
                    waits=[(sPE, womm_done[e_blk])],
                    inc=(sACT if tok == NT - 1 else None))
            woev_done[e_blk] = sACT.n

    hsl = [at("hsl", 64 * KB + i * 2 * KB, [128, TPC], BF16) for i in range(4)]
    dwstage = [at("dwstage", 72 * KB + i * 2 * KB, [128, 512], I32) for i in range(4)]
    dwb = [at("dwb", 80 * KB + i * KB, [128, 512], BF16) for i in range(4)]
    xsl = [at("xsl", 84 * KB + i * 16 * KB, [128, NT, 512], F32) for i in range(2)]
    t1b = at("t1b", 116 * KB, [128, NT, 512], F32)

    s7bar = snap()
    with nc.psum_tensor("dop", [128, NT, 512], F32) as dop_t:
        dop = dop_t.ap()
        dconv_done = {}
        dmmrow_done = {}
        dmm_done = [0] * 8
        dvet1_done = [0] * 8
        fin_done = {}
        yst_done = {}
        for o in range(8):
            ob = o % 2
            xw = [fin_done[(o - 2, NT - 1)]] if o >= 2 else list(s7bar)
            p.emit("sync", lambda e, o=o, ob=ob: e.dma_start(
                out=xsl[ob],
                in_=x_in.rearrange("(t p) d -> p t d", p=128)[:, :, o * 512:(o + 1) * 512]),
                waits=xw, inc=sXT[ob], inc_n=16)
            xsl_ld = (sXT[ob], sXT[ob].n)
            for f in range(FC):
                idx = o * FC + f
                fb = idx % 4
                w = [(sACT, dconv_done[idx - 4])] if idx >= 4 else (list(s7bar) if o == 0 else [])
                p.emit("sync", lambda e, o=o, f=f, fb=fb: e.dma_start(
                    out=dwstage[fb], in_=dw_in[o][f * 128:(f + 1) * 128, :]),
                    waits=w, inc=sD7[fb], inc_n=16)
                dld = (sD7[fb], sD7[fb].n)
                cw = [dld]
                if idx >= 4:
                    cw.append((sPE, dmmrow_done[idx - 4]))
                elif o == 0:
                    cw += list(s7bar)
                p.emit("scalar", lambda e, fb=fb: e.activation(
                    out=dwb[fb], in_=dwstage[fb], func=AF.Copy), waits=cw, inc=sACT)
                dconv_done[idx] = sACT.n
                hw = list(hall_done)
                if idx >= 4:
                    hw.append((sPE, dmmrow_done[idx - 4]))
                if o == 0 and f < 4:
                    hw += list(s7bar)
                p.emit("gpsimd", lambda e, f=f, fb=fb: e.dma_start(
                    out=hsl[fb], in_=hbuf[f]), waits=hw, inc=sH7[fb], inc_n=16)
                hld = (sH7[fb], sH7[fb].n)
                mw = [(sACT, dconv_done[idx]), hld]
                if f == 0 and o > 0:
                    mw.append((sDVE, dvet1_done[o - 1]))
                for tok in range(NT):
                    p.emit("tensor", lambda e, fb=fb, tok=tok, f=f: e.matmul(
                        dop[:, tok, :], hsl[fb][:, tok * 128:(tok + 1) * 128],
                        dwb[fb], start=(f == 0), stop=(f == FC - 1),
                        skip_group_check=True),
                        waits=mw if tok == 0 else (),
                        inc=(sPE if tok == NT - 1 else None))
                dmmrow_done[idx] = sPE.n
            dmm_done[o] = sPE.n
            for tok in range(NT):
                fw = [(sPE, dmm_done[o])]
                if o > 0:
                    fw.append(yst_done[(o - 1, tok)])
                p.emit("vector", lambda e, o=o, tok=tok: e.tensor_mul(
                    t1b[:, tok, :], dop[:, tok, :], dsb_s[:, o * 512:(o + 1) * 512]),
                    waits=fw, inc=(sDVE if tok == NT - 1 else None))
            dvet1_done[o] = sDVE.n
            for tok in range(NT):
                tb = tok % 2
                p.emit("vector", lambda e: e.drain())
                p.emit("vector", lambda e, o=o, tok=tok: e.tensor_add(
                    t1b[:, tok, :], t1b[:, tok, :],
                    attn_proj[:, tok, o * 512:(o + 1) * 512]),
                    waits=[xsl_ld])
                p.emit("vector", lambda e: e.drain())
                p.emit("vector", lambda e, o=o, tok=tok, ob=ob: e.tensor_add(
                    t1b[:, tok, :], t1b[:, tok, :], xsl[ob][:, tok, :]), inc=sDVE)
                fin_done[(o, tok)] = (sDVE, sDVE.n)
                p.emit("gpsimd", lambda e, o=o, tok=tok: e.dma_start(
                    out=y_out.rearrange("(t p) d -> p t d", p=128)[:, tok, o * 512:(o + 1) * 512],
                    in_=t1b[:, tok, :]),
                    waits=[(sDVE, sDVE.n)], inc=sST[tb], inc_n=16)
                yst_done[(o, tok)] = (sST[tb], sST[tb].n)
    p.emit("gpsimd", lambda e: e.nop(),
           waits=[(sST[0], sST[0].n), (sST[1], sST[1].n), (sSTB, sSTB.n)])

    with nc.Block() as block:
        @block.sync
        def _(eng):
            for f in p.ops["sync"]:
                f(eng)

        @block.tensor
        def _(eng):
            for f in p.ops["tensor"]:
                f(eng)

        @block.scalar
        def _(eng):
            for f in p.ops["scalar"]:
                f(eng)

        @block.vector
        def _(eng):
            for f in p.ops["vector"]:
                f(eng)

        @block.gpsimd
        def _(eng):
            for f in p.ops["gpsimd"]:
                f(eng)

    return nc


def _attn_plan():
    plans = []
    for j in range(2):
        amask = []
        free = []
        for sslot in range(NKT):
            am, fr = True, True
            for c in range(NCORE):
                kt = (8 * c + sslot) % NKT
                q0 = TPC * c + 512 * j
                kpos = np.arange(kt * 128, kt * 128 + 128)
                qpos = np.arange(q0, q0 + 512)
                valid = np.abs(kpos[:, None] - qpos[None, :]) >= WIN
                if valid.any():
                    am = False
                if not valid.all():
                    fr = False
            amask.append(am)
            free.append(fr)
        pairs = []
        mask_pairs = []
        for k in range(NKT // 2):
            s0, s1 = 2 * k, 2 * k + 1
            if amask[s0] and amask[s1]:
                continue
            if free[s0] and free[s1]:
                pairs.append((k, None))
            else:
                pairs.append((k, len(mask_pairs)))
                mask_pairs.append(k)
        plans.append((pairs, mask_pairs))
    return plans


_ATTN_PLANS = None


def _attn_plans():
    global _ATTN_PLANS
    if _ATTN_PLANS is None:
        _ATTN_PLANS = _attn_plan()
    return _ATTN_PLANS


def _prev_h(qt, g, hl, back):
    idx = (qt * KVH + g) * 4 + hl - back
    if idx < 0:
        return None
    q2, rem = divmod(idx, KVH * 4)
    g2, hl2 = divmod(rem, 4)
    return (q2, g2, hl2)


def _host_prep(L0_attn_hidden_states, L0_attn_Wq, L0_attn_Wk, L0_attn_Wv,
               L0_attn_Wo, L0_ffn_gate_w, L0_ffn_gate_s, L0_ffn_up_w,
               L0_ffn_up_s, L0_ffn_down_w, L0_ffn_down_s):
    x = np.ascontiguousarray(np.asarray(L0_attn_hidden_states, dtype=np.float32)[0])
    Wq = np.asarray(L0_attn_Wq, dtype=np.float32)
    Wk = np.asarray(L0_attn_Wk, dtype=np.float32)
    Wv = np.asarray(L0_attn_Wv, dtype=np.float32)
    Wo = np.asarray(L0_attn_Wo, dtype=np.float32)
    gw = np.asarray(L0_ffn_gate_w, dtype=np.int32)
    uw = np.asarray(L0_ffn_up_w, dtype=np.int32)
    dw = np.asarray(L0_ffn_down_w, dtype=np.int32)
    gs = np.asarray(L0_ffn_gate_s, dtype=np.float32)
    us = np.asarray(L0_ffn_up_s, dtype=np.float32)
    ds = np.asarray(L0_ffn_down_s, dtype=np.float32)

    wq_b = np.ascontiguousarray(Wq.T.reshape(E, HQ, 128).transpose(1, 0, 2))
    wk_b = np.ascontiguousarray(Wk.T.reshape(E, KVH, 128).transpose(1, 0, 2))
    wv_t = np.ascontiguousarray(Wv.T)
    wo_b = np.ascontiguousarray(Wo.T.reshape(E, 8, 512).transpose(1, 0, 2))
    gw_b = np.ascontiguousarray(gw.T.reshape(E, FC, 128).transpose(1, 0, 2))
    uw_b = np.ascontiguousarray(uw.T.reshape(E, FC, 128).transpose(1, 0, 2))
    dw_b = np.ascontiguousarray(dw.T.reshape(FF, 8, 512).transpose(1, 0, 2))
    gs_t = np.ascontiguousarray(gs.reshape(FC, 128).T)
    us_t = np.ascontiguousarray(us.reshape(FC, 128).T)
    dsb = np.ascontiguousarray(np.broadcast_to(ds[None, :], (128, E)))

    inv_freq = (1.0 / (10000.0 ** (np.arange(0, HD, 2, dtype=np.float32) / HD)))
    t = np.arange(S, dtype=np.float32)
    freqs = t[:, None] * inv_freq[None, :]
    cos_fm = np.ascontiguousarray(np.cos(freqs).T.astype(np.float32))
    sin_fm = np.ascontiguousarray(np.sin(freqs).T.astype(np.float32))

    plans = _attn_plans()
    in_maps = []
    for c in range(NCORE):
        m = np.zeros((2, 12, 128, 512), dtype=ml_dtypes.bfloat16)
        for qt in range(2):
            pairs, mask_pairs = plans[qt]
            qp = TPC * c + 512 * qt + np.arange(512)
            for mi, k in enumerate(mask_pairs):
                for jj in range(2):
                    sslot = 2 * k + jj
                    kt = (8 * c + sslot) % NKT
                    kpos = np.arange(kt * 128, kt * 128 + 128)
                    valid = (np.abs(kpos[:, None] - qp[None, :]) >= WIN)
                    m[qt, 2 * mi + jj] = valid.astype(np.float32)
        in_maps.append({
            "x": np.ascontiguousarray(x[TPC * c:TPC * (c + 1)]),
            "wq": wq_b, "wk": wk_b, "wv": wv_t, "wo": wo_b,
            "gw": gw_b, "uw": uw_b, "dw": dw_b,
            "gs": gs_t, "us": us_t, "dsb": dsb,
            "cos64": np.ascontiguousarray(cos_fm[:, TPC * c:TPC * (c + 1)]),
            "sin64": np.ascontiguousarray(sin_fm[:, TPC * c:TPC * (c + 1)]),
            "masks": m,
        })
    return in_maps


def kernel(**inputs):
    in_maps = _host_prep(**inputs)
    nc = build_nc()
    res = run_bass_kernel_spmd(nc, in_maps, core_ids=list(range(NCORE)))
    y = np.concatenate([res.results[c]["y"] for c in range(NCORE)], axis=0)
    return y.reshape(1, S, E)
